# version 08
# baseline (speedup 1.0000x reference)
import numpy as np
import concourse.bass as bass
import concourse.mybir as mybir
from concourse.bass_utils import run_bass_kernel_spmd

F32 = mybir.dt.float32
BF16 = mybir.dt.bfloat16
AF = mybir.ActivationFunctionType
ALU = mybir.AluOpType
AX = mybir.AxisListType

QUEUES = ("pe", "act", "dve", "pool", "sync")
_BUCKET = 2048


class TV:
    __slots__ = ("kind", "handle", "bname", "rowelems", "p0", "np_", "b0", "nb", "esz")

    def __init__(self, kind, handle, bname, rowelems, p0, np_, b0, nb, esz):
        self.kind, self.handle, self.bname = kind, handle, bname
        self.rowelems = rowelems
        self.p0, self.np_, self.b0, self.nb, self.esz = p0, np_, b0, nb, esz

    @property
    def ne(self):
        return self.nb // self.esz

    def __getitem__(self, idx):
        ps, es = idx
        p0 = ps.start or 0
        p1 = ps.stop if ps.stop is not None else self.np_
        e0 = es.start or 0
        e1 = es.stop if es.stop is not None else self.ne
        assert 0 <= p0 <= p1 <= self.np_ and 0 <= e0 <= e1 <= self.ne, (idx, self.np_, self.ne)
        return TV(self.kind, self.handle, self.bname, self.rowelems,
                  self.p0 + p0, p1 - p0, self.b0 + e0 * self.esz, (e1 - e0) * self.esz, self.esz)

    @property
    def shape(self):
        return (self.np_, self.ne)

    def ap(self):
        if self.kind == "dram":
            return bass.AP(self.handle, self.b0 // self.esz + self.p0 * self.rowelems,
                           [[self.rowelems, self.np_], [1, self.ne]])
        return self.handle[self.p0:self.p0 + self.np_,
                           self.b0 // self.esz: self.b0 // self.esz + self.ne]

    def ap3(self, dims):
        return bass.AP(self.handle, self.b0 // self.esz + self.p0 * self.rowelems, dims)

    def bcast(self, blk):
        return self.ap3([[self.rowelems, self.np_], [1, self.ne], [0, blk]])

    def colb(self, blk):
        assert self.ne == 1
        return self.ap3([[self.rowelems, self.np_], [0, blk]])

    def box(self):
        return (self.p0, self.p0 + self.np_, self.b0, self.b0 + self.nb)


def _overlap(a, b):
    return a[0] < b[1] and b[0] < a[1] and a[2] < b[3] and b[2] < a[3]


def _covers(a, b):
    return a[0] <= b[0] and a[1] >= b[1] and a[2] <= b[2] and a[3] >= b[3]


class Prog:
    def __init__(self, nc):
        self.nc = nc
        self.recs = {q: [] for q in QUEUES}
        self.count = {q: 0 for q in QUEUES}
        self.chan_count = {}
        self.acc = {}
        self.hwm = {q: {} for q in QUEUES}
        self._ctxs = []
        self._bitcache = {}
        self.n_inst = 0

    def sbuf(self, name, P, F, dtype):
        t = self.nc.sbuf_tensor(name, [P, F], dtype)
        h = t.__enter__()
        self._ctxs.append(t)
        esz = 2 if dtype == BF16 else 4
        return TV("sbuf", h, name, F, 0, P, 0, F * esz, esz)

    def carve(self, base, byteoff, nelems, dtype, p0=0, nparts=128):
        esz = 2 if dtype == BF16 else 4
        key = (base.bname, dtype)
        if key not in self._bitcache:
            self._bitcache[key] = base.handle if esz == base.esz else base.handle.bitcast(dtype)
        h = self._bitcache[key]
        rowbytes = base.rowelems * base.esz
        assert byteoff % esz == 0 and byteoff + nelems * esz <= rowbytes, (byteoff, nelems, esz)
        return TV("sbuf", h, base.bname, rowbytes // esz, p0, nparts, byteoff, nelems * esz, esz)

    def psum(self, name, P, F, dtype):
        t = self.nc.psum_tensor(name, [P, F], dtype)
        h = t.__enter__()
        self._ctxs.append(t)
        esz = 2 if dtype == BF16 else 4
        return TV("psum", h, name, F, 0, P, 0, F * esz, esz)

    def dram(self, name, R, C, dtype, kind="Internal"):
        ap = self.nc.dram_tensor(name, [R, C], dtype, kind=kind).ap()
        esz = 2 if dtype == BF16 else 4
        return TV("dram", ap.tensor, name, C, 0, R, 0, C * esz, esz)

    def _buckets(self, tv, b):
        if tv.kind == "psum":
            yield (tv.bname, 0)
            return
        for k in range(b[2] // _BUCKET, (b[3] - 1) // _BUCKET + 1):
            yield (tv.bname, k)

    def _dep_waits(self, q, reads, writes, extra=()):
        need = {}
        for tvs, wr in ((reads, False), (writes, True)):
            for tv in tvs:
                b = (0, 128, 0, 1) if tv.kind == "psum" else tv.box()
                rr = tv.kind == "psum"
                for key in self._buckets(tv, b):
                    for (bb, kind, lbl, v) in self.acc.get(key, ()):
                        if (wr or kind == "w" or (rr and lbl != q)) and _overlap(b, bb):
                            if q == "pe" and lbl == "pe":
                                continue
                            if v > need.get(lbl, 0):
                                need[lbl] = v
        for (lbl, v) in extra:
            if v > need.get(lbl, 0):
                need[lbl] = v
        out = []
        hw = self.hwm[q]
        for lbl, v in need.items():
            if hw.get(lbl, 0) >= v:
                continue
            hw[lbl] = v
            out.append((lbl, v))
        return out

    def _note(self, tv, kind, lbl, v):
        b = (0, 128, 0, 1) if tv.kind == "psum" else tv.box()
        for key in self._buckets(tv, b):
            lst = self.acc.setdefault(key, [])
            if kind == "w":
                lst[:] = [r for r in lst if not _covers(b, r[0])]
            else:
                lst[:] = [r for r in lst if not (r[1] == "r" and r[2] == lbl and _covers(b, r[0]))]
            lst.append((b, kind, lbl, v))

    def _record(self, q, fn, reads, writes, ch=None, n_inst=1):
        waits = self._dep_waits(q, reads, writes)
        if ch is None:
            self.count[q] += 1
            lbl, v = q, self.count[q]
            inc = (q, 1)
        else:
            c = self.chan_count.get(ch, 0) + 1
            self.chan_count[ch] = c
            lbl, v = "dma:" + ch, 16 * c
            inc = (lbl, 16)
        self.recs[q].append((waits, fn, inc))
        self.n_inst += n_inst + len(waits)
        for tv in reads:
            self._note(tv, "r", lbl, v)
        for tv in writes:
            self._note(tv, "w", lbl, v)
        return (lbl, v)

    def dma(self, ch, dst, src, q="sync", dst_ap=None, src_ap=None, **kw):
        d = dst_ap if dst_ap is not None else dst.ap()
        s = src_ap if src_ap is not None else src.ap()
        return self._record(q, lambda e: e.dma_start(d, s, **kw), [src], [dst], ch=ch)

    def mm_group(self, psum, ops, psum_ap=None):
        reads, aps = [], []
        for (lt, rt, la, ra) in ops:
            reads += [lt, rt]
            aps.append((la if la is not None else lt.ap(), ra if ra is not None else rt.ap()))
        pap = psum_ap if psum_ap is not None else psum.ap()
        n = len(ops)

        def fn(e):
            inst = None
            for i, (la, ra) in enumerate(aps):
                inst = e.matmul(pap, la, ra, start=(i == 0), stop=(i == n - 1))
            return inst
        return self._record("pe", fn, reads, [psum], n_inst=n)

    def mm(self, psum, lhsT, rhs, start=True, stop=True, lhsT_ap=None, rhs_ap=None, psum_ap=None):
        la = lhsT_ap if lhsT_ap is not None else lhsT.ap()
        ra = rhs_ap if rhs_ap is not None else rhs.ap()
        pap = psum_ap if psum_ap is not None else psum.ap()
        return self._record("pe", lambda e: e.matmul(pap, la, ra, start=start, stop=stop),
                            [lhsT, rhs], [psum])

    def transpose(self, psum, src, ident, psum_ap=None, src_ap=None):
        pap = psum_ap if psum_ap is not None else psum.ap()
        sap = src_ap if src_ap is not None else src.ap()
        return self._record("pe", lambda e: e.transpose(pap, sap, ident.ap()),
                            [src, ident], [psum])

    def tt(self, q, out, a, b, op, a_ap=None, b_ap=None):
        oap, aap = out.ap(), (a_ap if a_ap is not None else a.ap())
        bap = b_ap if b_ap is not None else b.ap()
        return self._record(q, lambda e: e.tensor_tensor(oap, aap, bap, op), [a, b], [out])

    def btt(self, q, out, a, colblk, blk, op, a_ap=None):
        return self.tt(q, out, a, colblk, op, a_ap=a_ap, b_ap=colblk.bcast(blk))

    def ts(self, q, out, a, s1, s2, op0, op1=None, a_ap=None):
        oap, aap = out.ap(), (a_ap if a_ap is not None else a.ap())
        reads = [a]
        s1v, s2v = s1, s2
        if isinstance(s1, TV):
            reads.append(s1); s1v = s1.ap()
        if isinstance(s2, TV):
            reads.append(s2); s2v = s2.ap()
        if op1 is None:
            return self._record(q, lambda e: e.tensor_scalar(oap, aap, s1v, None, op0), reads, [out])
        return self._record(q, lambda e: e.tensor_scalar(oap, aap, s1v, s2v, op0, op1), reads, [out])

    def stt(self, q, out, a, s, b, op0, op1, a_ap=None, b_ap=None):
        oap, aap = out.ap(), (a_ap if a_ap is not None else a.ap())
        bap = b_ap if b_ap is not None else b.ap()
        reads = [a, b]
        sv = s
        if isinstance(s, TV):
            reads.append(s); sv = s.ap()
        return self._record(q, lambda e: e.scalar_tensor_tensor(oap, aap, sv, bap, op0, op1),
                            reads, [out])

    def act(self, q, out, a, func, bias=None, scale=None, a_ap=None, accum_out=None):
        oap, aap = out.ap(), (a_ap if a_ap is not None else a.ap())
        reads, writes = [a], [out]
        kw = {}
        if bias is not None:
            if isinstance(bias, TV):
                reads.append(bias); kw["bias"] = bias.ap()
            else:
                kw["bias"] = bias
        if scale is not None:
            if isinstance(scale, TV):
                reads.append(scale); kw["scale"] = scale.ap()
            else:
                kw["scale"] = scale
        if accum_out is not None:
            writes.append(accum_out); kw["accum_out"] = accum_out.ap()
        return self._record(q, lambda e: e.activation(oap, aap, func, **kw), reads, writes)

    def copy(self, q, out, a, a_ap=None, out_ap=None):
        oap = out_ap if out_ap is not None else out.ap()
        aap = a_ap if a_ap is not None else a.ap()
        if q == "act":
            return self._record(q, lambda e: e.copy(oap, aap), [a], [out])
        return self._record(q, lambda e: e.tensor_copy(oap, aap), [a], [out])

    def reduce(self, q, out, a, op, a_ap=None):
        oap, aap = out.ap(), (a_ap if a_ap is not None else a.ap())
        return self._record(q, lambda e: e.tensor_reduce(oap, aap, AX.X, op), [a], [out])

    def recip(self, q, out, a):
        oap, aap = out.ap(), a.ap()
        return self._record(q, lambda e: e.reciprocal(oap, aap), [a], [out])

    def memset(self, q, out, val, out_ap=None):
        oap = out_ap if out_ap is not None else out.ap()
        return self._record(q, lambda e: e.memset(oap, val), [], [out])

    def raw(self, q, fn, reads, writes, ch=None):
        return self._record(q, fn, reads, writes, ch=ch)

    def chan_point(self, ch):
        return ("dma:" + ch, 16 * self.chan_count.get(ch, 0))

    def build(self, final_deps=()):
        nc = self.nc
        import contextlib
        stack = contextlib.ExitStack()
        sems = {}
        for q in QUEUES:
            sems[q] = stack.enter_context(nc.semaphore(f"s_{q}"))
        for ch in self.chan_count:
            sems["dma:" + ch] = stack.enter_context(nc.semaphore(f"s_{ch}"))
        block = stack.enter_context(nc.Block())

        def mk(qname):
            recs = self.recs[qname]

            def emit(eng):
                for (waits, fn, inc) in recs:
                    for (lbl, v) in waits:
                        eng.wait_ge(sems[lbl], v)
                    inst = fn(eng)
                    inst.then_inc(sems[inc[0]], inc[1])
                if qname == "sync":
                    for (lbl, v) in final_deps:
                        if v > 0 and lbl in sems:
                            eng.wait_ge(sems[lbl], v)
            return emit

        block.tensor(mk("pe"))
        block.scalar(mk("act"))
        block.vector(mk("dve"))
        block.gpsimd(mk("pool"))
        block.sync(mk("sync"))
        stack.close()
        return nc


O_BIG = 0
O_WST = 131072
O_WBF = 147456
O_DTM = 163840
O_SMALL = 167936
O_EVAC = 178176
O_NW = 186368
ARENA_F32 = 50688

B2_HBC = 0
B2_XBC = 24960
B2_STATE = 74112
B2_SBF = 90496
B2_XTM = 92544
B2_DTXS = 100736
B2_SCM = 102784
B2_YTB = 106880
B2_YN = 108928
B2_YT = 117120
B2_LB = 118144
B2_LFM = 120192
B2_HL = 120704
B2_MSK = 122368
B2_YXB = 128512


def build_mamba(L=2048, stop_after=None, skip=(), dbg=False):
    skip = set(skip)
    DM, II, CONV, PROJ = 4096, 4096, 6144, 10304
    NCH = L // 128
    NBLK = L // 256
    TT = min(512, L)
    NT = L // TT

    nc = bass.Bass("TRN2", target_bir_lowering=False)
    p = Prog(nc)

    xt = p.dram("xt", DM, L, F32, kind="ExternalInput")
    win_t = p.dram("win_t", 81 * 128, DM, F32, kind="ExternalInput")
    wout_t = p.dram("wout_t", 32 * 128, DM, F32, kind="ExternalInput")
    wup_t = p.dram("wup_t", 32 * 128, DM, F32, kind="ExternalInput")
    cw_in = p.dram("cw_in", 128, 192, F32, kind="ExternalInput")
    cb_in = p.dram("cb_in", 128, 48, F32, kind="ExternalInput")
    dtb = p.dram("dtb", 1, 64, F32, kind="ExternalInput")
    alog = p.dram("alog", 1, 64, F32, kind="ExternalInput")
    dvec = p.dram("dvec", 1, 64, F32, kind="ExternalInput")
    nw_in = p.dram("nw_in", 128, DM, F32, kind="ExternalInput")
    gwT = p.dram("gwT", DM, 4, F32, kind="ExternalInput")
    ebias = p.dram("ebias", 1, 4, F32, kind="ExternalInput")
    loraA_T = p.dram("loraA_T", DM, 32, F32, kind="ExternalInput")
    loraB_st = p.dram("loraB_st", 32, DM, F32, kind="ExternalInput")
    cst_in = p.dram("cst_in", 128, 512, F32, kind="ExternalInput")

    dk = "ExternalOutput" if dbg else "Internal"
    hbc_sp = p.dram("hbc_sp", CONV, L, BF16, kind=dk)
    gate_sp = p.dram("gate_sp", II, L, BF16, kind=dk)
    yn_sp = p.dram("yn_sp", L, II, BF16, kind=dk)
    attn_sp = p.dram("attn_sp", DM, L, BF16, kind=dk)
    h1f_sp = p.dram("h1f_sp", DM, L, F32, kind=dk)
    ffn_sp = p.dram("ffn_sp", DM, L, BF16, kind=dk)
    out_fm = p.dram("out_fm", DM, L, F32, kind="ExternalOutput")

    AR = p.sbuf("AR", 128, ARENA_F32, F32)
    PS = [p.psum(f"PS{i}", 128, 512, F32) for i in range(7)]
    PB = p.psum("PB", 128, 1024, BF16)
    PSA, PSB, PSC, PSD, PSE, PSF, PSG = PS

    cv = p.carve

    cst = cv(AR, O_SMALL, 512, F32)
    triu = cst[:, 0:128]
    id64 = cst[0:64, 256:320]
    selTf = cst[0:4, 320:352]
    ones128 = cst[:, 352:480]
    onesrow = ones128[0:1, :]
    id128b = cv(AR, O_SMALL + 2048, 128, BF16)
    selTb = cv(AR, O_SMALL + 2304, 32, BF16, 0, 4)
    onescolb = cv(AR, O_SMALL + 2368, 1, BF16)
    negA_rep = cv(AR, O_SMALL + 2432, 64, F32)
    dtb_rep = cv(AR, O_SMALL + 2688, 64, F32)
    D_rep = cv(AR, O_SMALL + 2944, 64, F32)
    eb_rep = cv(AR, O_SMALL + 3200, 4, F32)
    rowA = cv(AR, O_SMALL + 3216, 64, F32, 0, 1)
    lgt = cv(AR, O_SMALL + 3216, 4, F32)
    lgm = cv(AR, O_SMALL + 3232, 1, F32)
    gcol_r = [cv(AR, O_SMALL + 3236, 1, F32), cv(AR, O_SMALL + 3244, 1, F32)]
    gcol2_r = [cv(AR, O_SMALL + 3240, 1, F32), cv(AR, O_SMALL + 3248, 1, F32)]
    ohb = cv(AR, O_SMALL + 3264, 4, BF16)
    ohT = cv(AR, O_SMALL + 3272, 128, BF16, 0, 4)
    id64b = cv(AR, O_SMALL + 9728, 64, BF16, 0, 64)
    rowB = cv(AR, O_SMALL + 3584, 512, F32, 0, 1)
    rowC = cv(AR, O_SMALL + 5632, 512, F32, 0, 1)
    rowD = cv(AR, O_SMALL + 7680, 512, F32, 0, 1)

    p.dma("cst", cst, cst_in)
    p.copy("dve", id128b, cst[:, 128:256])
    p.copy("dve", id64b, id64)
    p.copy("dve", selTb, selTf)
    p.copy("dve", onescolb, cst[:, 479:480])
    for (dst, src, chn, ne) in ((negA_rep, alog, "c1", 64), (dtb_rep, dtb, "c2", 64),
                                (D_rep, dvec, "c3", 64), (eb_rep, ebias, "c4", 4)):
        tmp = rowA[0:1, 0:ne]
        p.dma(chn, tmp, src)
        if dst is negA_rep:
            p.act("act", tmp, tmp, AF.Exp)
            p.ts("dve", tmp, tmp, -1.0, None, ALU.mult)
        p.mm(PSG, cst, tmp, lhsT_ap=onesrow.ap(), rhs_ap=tmp.ap(),
             psum_ap=PSG.handle[0:128, 0:ne])
        p.copy("dve", dst, PSG[:, 0:ne])
    nwv = cv(AR, O_NW, DM, F32)
    p.dma("nw", nwv, nw_in)

    x_fm = cv(AR, O_BIG, 32 * 2048, BF16)
    wst = [cv(AR, O_WST, 2048, F32), cv(AR, O_WST + 8192, 2048, F32)]
    wbf = [cv(AR, O_WBF, 4096, BF16), cv(AR, O_WBF + 8192, 4096, BF16)]
    evac = [cv(AR, O_EVAC, 2048, BF16), cv(AR, O_EVAC + 4096, 2048, BF16)]
    dt_tm = cv(AR, O_DTM, NCH * 64, F32)

    xslots = [wst[0], wst[1], cv(AR, O_WBF, 2048, F32), cv(AR, O_WBF + 8192, 2048, F32)]
    for dc in range(32):
        st = xslots[dc % 4]
        p.dma(f"x{dc % 4}", st[:, 0:L], xt[dc * 128:(dc + 1) * 128, 0:L])
        p.copy(("dve", "pool")[dc % 2], x_fm[:, dc * 2048:dc * 2048 + L], st[:, 0:L])

    castq = {"n": 0}

    def cast_rr(out, a, a_ap=None, out_ap=None):
        q = ("dve", "pool")[castq["n"] % 2]
        castq["n"] += 1
        p.copy(q, out, a, a_ap=a_ap, out_ap=out_ap)

    def load_w_half(dram_w, tile, half, wslot, bfv, ch):
        st = wst[wslot]
        p.dma(ch, st[:, 0:2048], dram_w[tile * 128:(tile + 1) * 128,
                                        half * 2048:(half + 1) * 2048])
        cast_rr(bfv[:, half * 2048:(half + 1) * 2048], st[:, 0:2048])

    st = wst[0]
    src = bass.AP(win_t.handle, 80 * 128 * DM, [[DM, 128], [128, 32], [1, 64]])
    p.dma("w0", st[:, 0:2048], win_t, dst_ap=st.ap3([[st.rowelems, 128], [64, 32], [1, 64]]),
          src_ap=src)
    p.copy("dve", wbf[0][:, 0:2048], st[:, 0:2048])
    for tt in range(NCH):
        p.mm_group(PSE, [(x_fm[:, dc * 2048 + tt * 128: dc * 2048 + tt * 128 + 128],
                          wbf[0][:, dc * 64:(dc + 1) * 64], None, None) for dc in range(32)],
                   psum_ap=PSE.handle[0:128, 0:64])
        p.copy("act", dt_tm[:, tt * 64:(tt + 1) * 64], PSE[:, 0:64])

    def proj_fm(tile_base, n_cc, sp_dram, chp):
        for cc in range(n_cc):
            sl = cc % 2
            bfv = wbf[sl]
            for hf in range(2):
                load_w_half(win_t, tile_base + cc, hf, hf, bfv, f"{chp}{hf}")
            for it in range(NT):
                ps = PS[(cc * NT + it) % 4]
                p.mm_group(ps, [(bfv[:, dc * 128:(dc + 1) * 128],
                                 x_fm[:, dc * 2048 + it * TT: dc * 2048 + it * TT + TT],
                                 None, None) for dc in range(32)],
                           psum_ap=ps.handle[0:128, 0:TT])
                ev = evac[it % 2]
                p.copy("dve", ev[:, 0:TT], ps[:, 0:TT])
                p.dma(f"{chp}s{it % 2}", sp_dram[cc * 128:(cc + 1) * 128, it * TT:(it + 1) * TT],
                      ev[:, 0:TT], q="act")

    proj_fm(32, 48, hbc_sp, "h")
    proj_fm(0, 32, gate_sp, "g")

    if stop_after == "ph1":
        p.build(final_deps=[p.chan_point("hs0"), p.chan_point("hs1"),
                            p.chan_point("gs0"), p.chan_point("gs1")])
        return nc, p

    hbcs = cv(AR, O_BIG + B2_HBC, 48 * 260, BF16)
    xbc2 = [cv(AR, O_BIG + B2_XBC, 48 * 256, BF16),
            cv(AR, O_BIG + B2_XBC + 24576, 48 * 256, BF16)]
    state = cv(AR, O_BIG + B2_STATE, 4096, F32)
    sbf = [cv(AR, O_BIG + B2_SBF, 512, BF16), cv(AR, O_BIG + B2_SBF + 1024, 512, BF16)]
    x_tm = cv(AR, O_BIG + B2_XTM, 4096, BF16)
    dtxs_r = [cv(AR, O_BIG + B2_DTXS, 512, BF16), cv(AR, O_BIG + B2_DTXS + 1024, 512, BF16)]
    yn_tm = cv(AR, O_BIG + B2_YN, 4096, BF16)
    ytmp_r = [cv(AR, O_EVAC + 4096, 512, F32), cv(AR, O_BIG + B2_YTB, 512, F32)]
    yxd_r = [cv(AR, O_EVAC + 6144, 512, F32), cv(AR, O_BIG + B2_YXB, 512, F32)]
    LB = [cv(AR, O_BIG + B2_LB + i * 256, 64, F32) for i in range(8)]
    lb_x, lb_dtp, lb_a, lb_L, lb_eL, lb_eS, lb_dc, lb_ca = LB
    l_fm = cv(AR, O_BIG + B2_LFM, 128, F32)
    hi_fm = cv(AR, O_BIG + B2_HL, 128, BF16, 0, 64)
    lo_fm = cv(AR, O_BIG + B2_HL + 256, 128, BF16, 0, 64)
    hib = cv(AR, O_BIG + B2_HL + 512, 64, BF16)
    lob = cv(AR, O_BIG + B2_HL + 640, 64, BF16)
    Lhl = cv(AR, O_BIG + B2_HL + 768, 64, F32)
    neghi = cv(AR, O_BIG + B2_HL + 1024, 128, BF16, 0, 64)
    neglo = cv(AR, O_BIG + B2_HL + 1280, 128, BF16, 0, 64)
    dm_w = [cv(AR, O_BIG + B2_MSK, 512, F32), cv(AR, O_BIG + B2_MSK + 2048, 512, F32)]
    m_w = [cv(AR, O_BIG + B2_MSK + 4096, 512, BF16), cv(AR, O_BIG + B2_MSK + 5120, 512, BF16)]
    scm_r = [cv(AR, O_BIG + B2_SCM + i * 256, 128, BF16) for i in range(16)]
    scg_r = [cv(AR, O_BIG + B2_YT, 128, BF16), cv(AR, O_BIG + B2_YT + 256, 128, BF16)]
    btm_r = [cv(AR, O_BIG + B2_YT + 512, 128, BF16), cv(AR, O_BIG + B2_YT + 768, 128, BF16)]
    cw_s = cv(AR, O_EVAC, 192, F32)
    cb_s = cv(AR, O_EVAC + 1024, 48, F32)
    cacc = [cv(AR, O_EVAC + 2048, 256, F32), cv(AR, O_EVAC + 3072, 256, F32)]

    p.dma("cw", cw_s, cw_in)
    p.dma("cb", cb_s, cb_in)
    p.memset("pool", state, 0.0)

    def load_hbc(blk):
        t0 = blk * 256
        hs = hbcs
        if blk == 0:
            p.memset("dve", hs, 0.0,
                     out_ap=hs.ap3([[hs.rowelems, 128], [260, 48], [1, 4]]))
            src_ = bass.AP(hbc_sp.handle, 0, [[L, 128], [128 * L, 48], [1, 256]])
            dst = bass.AP(hs.handle, hs.b0 // 2 + 3, [[hs.rowelems, 128], [260, 48], [1, 256]])
            p.dma("hb", hs, hbc_sp, dst_ap=dst, src_ap=src_)
        else:
            src_ = bass.AP(hbc_sp.handle, t0 - 3, [[L, 128], [128 * L, 48], [1, 259]])
            dst = hs.ap3([[hs.rowelems, 128], [260, 48], [1, 259]])
            p.dma("hb", hs, hbc_sp, dst_ap=dst, src_ap=src_)

    def conv_cc(blk, cc):
        if "conv" in skip:
            return
        hs, xbc = hbcs, xbc2[blk % 2]
        ac = cacc[cc % 2]
        for k in range(4):
            sh_ap = bass.AP(hs.handle, hs.b0 // 2 + cc * 260 + k,
                            [[hs.rowelems, 128], [1, 256]])
            if k == 0:
                p.stt("dve", ac, hs, cw_s[:, cc * 4:cc * 4 + 1], cb_s[:, cc:cc + 1],
                      ALU.mult, ALU.add, a_ap=sh_ap, b_ap=cb_s[:, cc:cc + 1].colb(256))
            else:
                p.stt("dve", ac, hs, cw_s[:, cc * 4 + k:cc * 4 + k + 1], ac,
                      ALU.mult, ALU.add, a_ap=sh_ap)
        p.act("act", xbc[:, cc * 256:(cc + 1) * 256], ac, AF.Silu)

    load_hbc(0)
    for cc in range(48):
        conv_cc(0, cc)

    for blk in range(NBLK):
        if blk + 1 < NBLK:
            load_hbc(blk + 1)
        xbc = xbc2[blk % 2]
        for sub in range(2):
            c = blk * 2 + sub
            xo = sub * 128
            dts = dt_tm[:, c * 64:(c + 1) * 64]
            if "lprep" in skip:
                for lb in LB:
                    p.memset("dve", lb, 0.01)
                p.memset("dve", l_fm, 0.01)
                p.memset("dve", Lhl, 0.01)
                p.memset("dve", hi_fm, 0.01)
                p.memset("dve", lo_fm, 0.0)
                p.memset("dve", neghi, -0.01)
                p.memset("dve", neglo, 0.0)
            else:
                p.tt("dve", lb_x, dts, dtb_rep, ALU.add)
                p.act("act", lb_a, lb_x, AF.Abs)
                p.ts("dve", lb_dc, lb_a, -1.0, None, ALU.mult)
                p.act("act", lb_a, lb_dc, AF.Exp)
                p.ts("dve", lb_dc, lb_a, 1.0, None, ALU.add)
                p.act("act", lb_a, lb_dc, AF.Ln)
                p.ts("dve", lb_dtp, lb_x, 0.0, None, ALU.max)
                p.tt("dve", lb_dtp, lb_dtp, lb_a, ALU.add)
                p.tt("dve", lb_a, lb_dtp, negA_rep, ALU.mult)
                p.mm(PSE, triu, lb_a, psum_ap=PSE.handle[0:128, 0:64])
                p.mm(PSE, lb_a, triu, psum_ap=PSE.handle[0:64, 64:192])
                p.mm(PSE, ones128, lb_a, psum_ap=PSE.handle[0:128, 192:256])
                p.copy("dve", lb_L, PSE[:, 0:64])
                p.copy("dve", l_fm[0:64, :], PSE[0:64, 64:192])
                p.copy("dve", lb_x, PSE[:, 192:256])
                p.act("act", lb_eL, lb_L, AF.Exp)
                p.act("act", lb_eS, lb_x, AF.Exp)
                p.tt("dve", lb_dc, lb_x, lb_L, ALU.subtract)
                p.act("act", lb_dc, lb_dc, AF.Exp)
                p.tt("dve", lb_ca, lb_dc, lb_dtp, ALU.mult)
                p.copy("dve", hi_fm, l_fm[0:64, :])
                p.tt("dve", lo_fm, l_fm[0:64, :], hi_fm, ALU.subtract)
                p.ts("dve", neghi, hi_fm, -1.0, None, ALU.mult)
                p.ts("dve", neglo, lo_fm, -1.0, None, ALU.mult)
                p.copy("dve", hib, lb_L)
                p.tt("dve", lob, lb_L, hib, ALU.subtract)
                p.tt("dve", Lhl, hib, lob, ALU.add)

            def stage_a(g):
                if blk + 1 < NBLK:
                    for k3 in range(3):
                        conv_cc(blk + 1, (sub * 8 + g) * 3 + k3)
                if "xtm" not in skip:
                    for i4 in range(4):
                        cc = g * 4 + i4
                        p.transpose(PB, xbc[:, cc * 256 + xo: cc * 256 + xo + 128], id128b,
                                    psum_ap=PB.handle[0:128, i4 * 128:(i4 + 1) * 128])
                    for i4 in range(4):
                        cc = g * 4 + i4
                        p.copy("act", x_tm[:, cc * 128:(cc + 1) * 128],
                               PB[:, i4 * 128:(i4 + 1) * 128])
                else:
                    p.memset("dve", x_tm[:, g * 512:(g + 1) * 512], 0.01)
                if "dtxs" not in skip:
                    p.btt("pool", dtxs_r[g % 2], x_tm[:, g * 512:(g + 1) * 512],
                          lb_ca[:, g * 8:(g + 1) * 8], 64, ALU.mult)
                else:
                    p.memset("pool", dtxs_r[g % 2], 0.01)
                B_fm = xbc[:, (32 + g) * 256 + xo:(32 + g) * 256 + xo + 128]
                C_fm = xbc[:, (40 + g) * 256 + xo:(40 + g) * 256 + xo + 128]
                scg = scg_r[g % 2]
                if "scores" not in skip:
                    p.mm(PSE, B_fm, C_fm, psum_ap=PSE.handle[0:128, 256:384])
                    p.tt("dve", scg, PSE[:, 256:384], triu, ALU.mult)
                else:
                    p.memset("dve", scg, 0.01)
                if "btm" not in skip:
                    bslot = 4 + g % 2
                    p.transpose(PB, B_fm, id128b,
                                psum_ap=PB.handle[0:128, bslot * 128:(bslot + 1) * 128])
                    p.copy("act", btm_r[g % 2], PB[:, bslot * 128:(bslot + 1) * 128])
                else:
                    p.memset("dve", btm_r[g % 2], 0.01)
                if "scast" not in skip:
                    p.copy("pool", sbf[g % 2], state[:, g * 512:(g + 1) * 512])
                else:
                    p.memset("dve", sbf[g % 2], 0.01)
                if "masks" not in skip:
                    for jb in range(2):
                        dps = (PSC, PSD)[jb]
                        for j4 in range(4):
                            h = g * 8 + jb * 4 + j4
                            col = bass.AP(id64b.handle, id64b.b0 // 2 + h,
                                          [[id64b.rowelems, 64], [0, 128]])
                            p.mm_group(dps, [(id64b, hi_fm, col, None),
                                             (neghi, id64b, None, col),
                                             (id64b, lo_fm, col, None),
                                             (neglo, id64b, None, col)],
                                       psum_ap=dps.handle[0:128, j4 * 128:(j4 + 1) * 128])
                        dmw = dm_w[jb]
                        p.ts("dve", dmw, dps[:, 0:512], 0.0, None, ALU.min)
                        mw = m_w[jb]
                        p.act("act", mw, dmw, AF.Exp)
                        for j4 in range(4):
                            j = jb * 4 + j4
                            h = g * 8 + j
                            p.stt("dve", scm_r[(g % 2) * 8 + j], scg,
                                  lb_dtp[:, h:h + 1], mw[:, j4 * 128:(j4 + 1) * 128],
                                  ALU.mult, ALU.mult,
                                  b_ap=mw[:, j4 * 128:(j4 + 1) * 128].ap())

            def stage_b(g):
                ytmp, yxd = ytmp_r[g % 2], yxd_r[g % 2]
                gcol, gcol2 = gcol_r[g % 2], gcol2_r[g % 2]
                C_fm = xbc[:, (40 + g) * 256 + xo:(40 + g) * 256 + xo + 128]
                if "inter" not in skip:
                    p.mm(PSB, C_fm, sbf[g % 2])
                if "intra" not in skip:
                    for j in range(8):
                        h = g * 8 + j
                        scm = scm_r[(g % 2) * 8 + j] if "masks" not in skip else scg_r[g % 2]
                        p.mm(PSA, scm, x_tm[:, h * 64:(h + 1) * 64],
                             psum_ap=PSA.handle[0:128, j * 64:(j + 1) * 64])
                if "ycomb" not in skip:
                    p.btt("dve", ytmp, PSB, lb_eL[:, g * 8:(g + 1) * 8], 64, ALU.mult)
                    p.tt("dve", ytmp, PSA, ytmp, ALU.add)
                    p.btt("pool", yxd, x_tm[:, g * 512:(g + 1) * 512],
                          D_rep[:, g * 8:(g + 1) * 8], 64, ALU.mult)
                    p.tt("pool", yxd, ytmp, yxd, ALU.add)
                if "gn" not in skip:
                    p.act("act", ytmp, yxd, AF.Square, accum_out=gcol)
                    p.ts("pool", gcol2, gcol, 1.0 / 512, 1e-5, ALU.mult, ALU.add)
                    p.act("act", gcol2, gcol2, AF.Ln)
                    p.ts("pool", gcol2, gcol2, -0.5, None, ALU.mult)
                    p.act("act", gcol2, gcol2, AF.Exp)
                    p.tt("pool", ytmp, yxd, gcol2, ALU.mult, b_ap=gcol2.colb(512))
                    p.tt("pool", yn_tm[:, g * 512:(g + 1) * 512], ytmp,
                         nwv[:, g * 512:(g + 1) * 512], ALU.mult)
                else:
                    p.memset("dve", yn_tm[:, g * 512:(g + 1) * 512], 0.0)
                if "supd" not in skip:
                    p.mm(PSF, btm_r[g % 2], dtxs_r[g % 2])
                    p.copy("act", ytmp, PSF)
                    p.btt("pool", yxd, state[:, g * 512:(g + 1) * 512],
                          lb_eS[:, g * 8:(g + 1) * 8], 64, ALU.mult)
                    p.tt("dve", state[:, g * 512:(g + 1) * 512], yxd, ytmp, ALU.add)

            stage_a(0)
            for g in range(1, 8):
                stage_a(g)
                stage_b(g - 1)
            stage_b(7)
            p.dma(f"yn{c % 2}", yn_sp[c * 128:(c + 1) * 128, :], yn_tm, q="act")

    if stop_after == "ph2":
        p.build(final_deps=[p.chan_point("yn0"), p.chan_point("yn1")])
        return nc, p

    ygt = cv(AR, O_BIG, 32 * 2048, BF16)
    gtile = [cv(AR, O_NW + 4096 * i, 2048, BF16) for i in range(4)]
    ytile = [cv(AR, O_WST + 4096 * i, 2048, BF16) for i in range(4)]
    for ic in range(32):
        s4 = ic % 4
        gt, yt = gtile[s4], ytile[s4]
        p.dma(f"gg{s4}", gt[:, 0:L], gate_sp[ic * 128:(ic + 1) * 128, :], q="sync")
        p.raw("sync",
              lambda e, d=yt[:, 0:L].ap(), s=yn_sp[0:L, ic * 128:(ic + 1) * 128].ap():
              e.dma_start(d, s, transpose=True), [yn_sp], [yt], ch=f"y{s4}")
        p.act("act", gt[:, 0:L], gt[:, 0:L], AF.Silu)
        p.tt(("dve", "pool")[ic % 2], ygt[:, ic * 2048:ic * 2048 + L], gt[:, 0:L],
             yt[:, 0:L], ALU.mult)

    def big_mm_phase(w_dram, rhs_big, sp_dram, chp, extra_ops=None, ev_slots=None):
        evs = ev_slots if ev_slots is not None else evac
        for oc in range(32):
            sl = oc % 2
            bfv = wbf[sl]
            for hf in range(2):
                load_w_half(w_dram, oc, hf, hf, bfv, f"w{hf}")
            for it in range(NT):
                ps = PS[(oc * NT + it) % 4]
                ops = [(bfv[:, dc * 128:(dc + 1) * 128],
                        rhs_big[:, dc * 2048 + it * TT: dc * 2048 + it * TT + TT], None, None)
                       for dc in range(32)]
                if extra_ops is not None:
                    ops.append(extra_ops(oc, it))
                p.mm_group(ps, ops, psum_ap=ps.handle[0:128, 0:TT])
                ev = evs[it % 2]
                p.copy("dve", ev[:, 0:TT], ps[:, 0:TT])
                p.dma(f"{chp}{it % 2}", sp_dram[oc * 128:(oc + 1) * 128, it * TT:(it + 1) * TT],
                      ev[:, 0:TT], q="act")

    big_mm_phase(wout_t, ygt, attn_sp, "as")

    h1b = cv(AR, O_BIG, 32 * 2048, BF16)
    xstg = [cv(AR, O_WST + i * 2048, 512, F32) for i in range(4)]
    astg = [cv(AR, O_WST + 8192 + i * 1024, 512, BF16) for i in range(4)]
    xres4 = [cv(AR, O_WST + 12288 + i * 2048, 512, F32) for i in range(2)]
    xres4 += [cv(AR, O_WBF + i * 2048, 512, F32) for i in range(2)]
    mrep = cv(AR, O_WBF + 4096, 512, F32)
    rrep = cv(AR, O_WBF + 6144, 512, F32)
    xrbf = [cv(AR, O_WBF + 8192 + i * 1024, 512, BF16) for i in range(4)]

    TT2 = min(512, L)
    NT2 = L // TT2

    def ln_phase(x_dram, add_dram, out_f32_dram, out_bf, tagc, after_tile=None):
        for it in range(NT2):
            t1 = it * TT2
            for icc in range(32):
                s4 = icc % 4
                q = ("dve", "pool")[icc % 2]
                xs, as_, xr = xstg[s4], astg[s4], xres4[s4]
                p.dma(f"lx{s4}", xs[:, 0:TT2], x_dram[icc * 128:(icc + 1) * 128, t1:t1 + TT2])
                p.dma(f"la{s4}", as_[:, 0:TT2],
                      add_dram[icc * 128:(icc + 1) * 128, t1:t1 + TT2])
                p.tt(q, xr[:, 0:TT2], xs[:, 0:TT2], as_[:, 0:TT2], ALU.add)
                p.copy(q, xrbf[s4][:, 0:TT2], xr[:, 0:TT2])
                p.mm(PSE, onescolb, xrbf[s4][:, 0:TT2], start=(icc == 0), stop=(icc == 31),
                     psum_ap=PSE.handle[0:1, 0:TT2])
                p.act("act", as_[:, 0:TT2], xr[:, 0:TT2], AF.Square)
                p.mm(PSF, onescolb, as_[:, 0:TT2], start=(icc == 0), stop=(icc == 31),
                     psum_ap=PSF.handle[0:1, 0:TT2])
            p.ts("dve", rowB[:, 0:TT2], PSE[0:1, 0:TT2], 1.0 / 4096, None, ALU.mult)
            p.ts("dve", rowC[:, 0:TT2], PSF[0:1, 0:TT2], 1.0 / 4096, None, ALU.mult)
            p.tt("dve", rowD[:, 0:TT2], rowB[:, 0:TT2], rowB[:, 0:TT2], ALU.mult)
            p.tt("dve", rowC[:, 0:TT2], rowC[:, 0:TT2], rowD[:, 0:TT2], ALU.subtract)
            p.ts("dve", rowC[:, 0:TT2], rowC[:, 0:TT2], 1e-5, None, ALU.add)
            p.act("act", rowC[:, 0:TT2], rowC[:, 0:TT2], AF.Sqrt)
            p.recip("dve", rowC[:, 0:TT2], rowC[:, 0:TT2])
            p.mm(PSG, cst, rowB, lhsT_ap=onesrow.ap(), rhs_ap=rowB[:, 0:TT2].ap(),
                 psum_ap=PSG.handle[0:128, 0:TT2])
            p.copy("dve", mrep[:, 0:TT2], PSG[:, 0:TT2])
            p.mm(PSG, cst, rowC, lhsT_ap=onesrow.ap(), rhs_ap=rowC[:, 0:TT2].ap(),
                 psum_ap=PSG.handle[0:128, 0:TT2])
            p.copy("dve", rrep[:, 0:TT2], PSG[:, 0:TT2])
            for icc in range(32):
                s4 = icc % 4
                q = ("dve", "pool")[icc % 2]
                xs, as_, xr = xstg[s4], astg[s4], xres4[s4]
                p.dma(f"lx{s4}", xs[:, 0:TT2], x_dram[icc * 128:(icc + 1) * 128, t1:t1 + TT2])
                p.dma(f"la{s4}", as_[:, 0:TT2],
                      add_dram[icc * 128:(icc + 1) * 128, t1:t1 + TT2])
                p.tt(q, xr[:, 0:TT2], xs[:, 0:TT2], as_[:, 0:TT2], ALU.add)
                p.tt(q, xr[:, 0:TT2], xr[:, 0:TT2], mrep[:, 0:TT2], ALU.subtract)
                p.tt(q, xr[:, 0:TT2], xr[:, 0:TT2], rrep[:, 0:TT2], ALU.mult)
                if out_f32_dram is not None:
                    p.dma(f"lo{s4}", out_f32_dram[icc * 128:(icc + 1) * 128, t1:t1 + TT2],
                          xr[:, 0:TT2], q="act")
                    p.copy("act", out_bf[:, icc * 2048 + t1: icc * 2048 + t1 + TT2],
                           xr[:, 0:TT2])
                else:
                    p.dma(f"{tagc}{s4}", out_bf[icc * 128:(icc + 1) * 128, t1:t1 + TT2],
                          xr[:, 0:TT2], q="act")
            if after_tile is not None:
                after_tile(it, t1)

    astT = cv(AR, O_DTM, 1024, BF16)
    wgb = cv(AR, O_DTM + 2048, 128, BF16)
    bstT = cv(AR, O_NW, 4096, BF16, 0, 32)
    wstB = cv(AR, O_WST, 4096, F32, 0, 32)
    zb = cv(AR, O_EVAC, 2048, BF16, 0, 32)
    ohrep = cv(AR, O_EVAC + 4096, 2048, BF16, 0, 32)
    zm = cv(AR, O_NW + 8192, 2048, BF16, 0, 32)
    sm_ev = [cv(AR, O_NW + 12288, 512, BF16), cv(AR, O_NW + 13312, 512, BF16)]

    st0 = wst[0]
    p.dma("w0", st0[:, 0:1024], loraA_T,
          dst_ap=st0.ap3([[st0.rowelems, 128], [32, 32], [1, 32]]),
          src_ap=bass.AP(loraA_T.handle, 0, [[32, 128], [128 * 32, 32], [1, 32]]))
    p.copy("dve", astT, st0[:, 0:1024])
    st1 = wst[1]
    p.dma("w1", st1[:, 0:128], gwT,
          dst_ap=st1.ap3([[st1.rowelems, 128], [4, 32], [1, 4]]),
          src_ap=bass.AP(gwT.handle, 0, [[4, 128], [128 * 4, 32], [1, 4]]))
    p.copy("dve", wgb, st1[:, 0:128])
    p.dma("w0", wstB, loraB_st)
    p.copy("dve", bstT, wstB)

    def ffn_head(it, t1):
        TTl = TT2
        p.mm_group(PSE, [(astT[:, dc * 32:(dc + 1) * 32],
                          h1b[:, dc * 2048 + t1: dc * 2048 + t1 + TTl], None, None)
                         for dc in range(32)],
                   psum_ap=PSE.handle[0:32, 0:TTl])
        p.copy("act", zb[:, t1:t1 + TTl], PSE[0:32, 0:TTl])
        for tts in range(t1 // 128, (t1 + TTl) // 128):
            p.mm_group(PSF, [(h1b[:, dc * 2048 + tts * 128: dc * 2048 + tts * 128 + 128],
                              wgb[:, dc * 4:(dc + 1) * 4], None, None) for dc in range(32)],
                       psum_ap=PSF.handle[0:128, 0:4])
            p.tt("dve", lgt, PSF[:, 0:4], eb_rep, ALU.add)
            p.reduce("dve", lgm, lgt, ALU.max)
            p.ts("dve", ohb, lgt, lgm[:, 0:1], None, ALU.is_ge)
            p.transpose(PB, ohb, id128b, psum_ap=PB.handle[0:4, 0:128])
            p.copy("act", ohT, PB[0:4, 0:128])
            p.mm(PSG, selTb, ohT, psum_ap=PSG.handle[0:32, 0:128])
            p.copy("act", ohrep[:, tts * 128:(tts + 1) * 128], PSG[0:32, 0:128])
            p.tt("dve", zm[:, tts * 128:(tts + 1) * 128], zb[:, tts * 128:(tts + 1) * 128],
                 ohrep[:, tts * 128:(tts + 1) * 128], ALU.mult)

    ln_phase(xt, attn_sp, h1f_sp, h1b, "lo", after_tile=ffn_head)

    def lora_extra(oc, it):
        return (bstT[0:32, oc * 128:(oc + 1) * 128], zm[0:32, it * TT:(it + 1) * TT],
                None, None)

    big_mm_phase(wup_t, h1b, ffn_sp, "fs", extra_ops=lora_extra, ev_slots=sm_ev)

    ln_phase(h1f_sp, ffn_sp, None, out_fm, "o")

    p.build(final_deps=[p.chan_point(f"o{i}") for i in range(4)])
    return nc, p


def tile_w(wT, ntile):
    wT = np.asarray(wT, np.float32)
    out_dim = wT.shape[1]
    pad = ntile * 128 - out_dim
    if pad:
        wT = np.concatenate([wT, np.zeros((wT.shape[0], pad), np.float32)], axis=1)
    r = wT.reshape(32, 128, ntile, 128).transpose(2, 1, 0, 3)
    return np.ascontiguousarray(r.reshape(ntile * 128, 4096))


def make_consts():
    consts = np.zeros((128, 512), np.float32)
    ii, jj = np.arange(128)[:, None], np.arange(128)[None, :]
    consts[:, 0:128] = (ii <= jj)
    consts[:, 128:256] = np.eye(128)
    consts[0:64, 256:320] = np.eye(64)
    for e in range(4):
        consts[e, 320 + 8 * e:320 + 8 * e + 8] = 1.0
    consts[:, 352:480] = 1.0
    return consts


def kernel(**inp):
    a = {k: np.asarray(v, dtype=np.float32) for k, v in inp.items()}
    x = a["L0_attn_hidden_states"]
    B, L, DM = x.shape
    cw = a["L0_attn_conv_weight"]
    cb = a["L0_attn_conv_bias"]
    consts = make_consts()

    common = dict(
        win_t=tile_w(a["L0_attn_in_proj_w"].T, 81),
        wout_t=tile_w(a["L0_attn_out_proj_w"].T, 32),
        wup_t=tile_w(a["L0_ffn_W_up"].T, 32),
        cw_in=np.ascontiguousarray(cw.reshape(48, 128, 4).transpose(1, 0, 2).reshape(128, 192)),
        cb_in=np.ascontiguousarray(cb.reshape(48, 128).T),
        dtb=a["L0_attn_dt_bias"].reshape(1, 64),
        alog=a["L0_attn_A_log"].reshape(1, 64),
        dvec=a["L0_attn_D"].reshape(1, 64),
        nw_in=np.ascontiguousarray(np.broadcast_to(a["L0_attn_norm_weight"], (128, DM))),
        gwT=np.ascontiguousarray(a["L0_ffn_gate_w"].T),
        ebias=a["L0_ffn_expert_bias"].reshape(1, 4),
        loraA_T=np.ascontiguousarray(a["L0_ffn_lora_A"].reshape(32, DM).T),
        loraB_st=np.ascontiguousarray(a["L0_ffn_lora_B"].transpose(0, 2, 1).reshape(32, DM)),
        cst_in=consts,
    )
    nc, _ = build_mamba(L)
    in_maps = [dict(common, xt=np.ascontiguousarray(x[b].T)) for b in range(B)]
    res = run_bass_kernel_spmd(nc, in_maps, core_ids=list(range(B)))
    out = np.stack([np.ascontiguousarray(res.results[b]["out_fm"]).T for b in range(B)])
    return np.ascontiguousarray(out, dtype=np.float32)


# version 09
# speedup vs baseline: 1.0340x; 1.0340x over previous
import numpy as np
import concourse.bass as bass
import concourse.mybir as mybir
from concourse.bass_utils import run_bass_kernel_spmd

F32 = mybir.dt.float32
BF16 = mybir.dt.bfloat16
AF = mybir.ActivationFunctionType
ALU = mybir.AluOpType
AX = mybir.AxisListType

QUEUES = ("pe", "act", "dve", "pool", "sync")
_BUCKET = 2048


class TV:
    __slots__ = ("kind", "handle", "bname", "rowelems", "p0", "np_", "b0", "nb", "esz")

    def __init__(self, kind, handle, bname, rowelems, p0, np_, b0, nb, esz):
        self.kind, self.handle, self.bname = kind, handle, bname
        self.rowelems = rowelems
        self.p0, self.np_, self.b0, self.nb, self.esz = p0, np_, b0, nb, esz

    @property
    def ne(self):
        return self.nb // self.esz

    def __getitem__(self, idx):
        ps, es = idx
        p0 = ps.start or 0
        p1 = ps.stop if ps.stop is not None else self.np_
        e0 = es.start or 0
        e1 = es.stop if es.stop is not None else self.ne
        assert 0 <= p0 <= p1 <= self.np_ and 0 <= e0 <= e1 <= self.ne, (idx, self.np_, self.ne)
        return TV(self.kind, self.handle, self.bname, self.rowelems,
                  self.p0 + p0, p1 - p0, self.b0 + e0 * self.esz, (e1 - e0) * self.esz, self.esz)

    @property
    def shape(self):
        return (self.np_, self.ne)

    def ap(self):
        if self.kind == "dram":
            return bass.AP(self.handle, self.b0 // self.esz + self.p0 * self.rowelems,
                           [[self.rowelems, self.np_], [1, self.ne]])
        return self.handle[self.p0:self.p0 + self.np_,
                           self.b0 // self.esz: self.b0 // self.esz + self.ne]

    def ap3(self, dims):
        return bass.AP(self.handle, self.b0 // self.esz + self.p0 * self.rowelems, dims)

    def bcast(self, blk):
        return self.ap3([[self.rowelems, self.np_], [1, self.ne], [0, blk]])

    def colb(self, blk):
        assert self.ne == 1
        return self.ap3([[self.rowelems, self.np_], [0, blk]])

    def box(self):
        return (self.p0, self.p0 + self.np_, self.b0, self.b0 + self.nb)


def _overlap(a, b):
    return a[0] < b[1] and b[0] < a[1] and a[2] < b[3] and b[2] < a[3]


def _covers(a, b):
    return a[0] <= b[0] and a[1] >= b[1] and a[2] <= b[2] and a[3] >= b[3]


class Prog:
    def __init__(self, nc):
        self.nc = nc
        self.recs = {q: [] for q in QUEUES}
        self.count = {q: 0 for q in QUEUES}
        self.chan_count = {}
        self.acc = {}
        self.hwm = {q: {} for q in QUEUES}
        self._ctxs = []
        self._bitcache = {}
        self.n_inst = 0

    def sbuf(self, name, P, F, dtype):
        t = self.nc.sbuf_tensor(name, [P, F], dtype)
        h = t.__enter__()
        self._ctxs.append(t)
        esz = 2 if dtype == BF16 else 4
        return TV("sbuf", h, name, F, 0, P, 0, F * esz, esz)

    def carve(self, base, byteoff, nelems, dtype, p0=0, nparts=128):
        esz = 2 if dtype == BF16 else 4
        key = (base.bname, dtype)
        if key not in self._bitcache:
            self._bitcache[key] = base.handle if esz == base.esz else base.handle.bitcast(dtype)
        h = self._bitcache[key]
        rowbytes = base.rowelems * base.esz
        assert byteoff % esz == 0 and byteoff + nelems * esz <= rowbytes, (byteoff, nelems, esz)
        return TV("sbuf", h, base.bname, rowbytes // esz, p0, nparts, byteoff, nelems * esz, esz)

    def psum(self, name, P, F, dtype):
        t = self.nc.psum_tensor(name, [P, F], dtype)
        h = t.__enter__()
        self._ctxs.append(t)
        esz = 2 if dtype == BF16 else 4
        return TV("psum", h, name, F, 0, P, 0, F * esz, esz)

    def dram(self, name, R, C, dtype, kind="Internal"):
        ap = self.nc.dram_tensor(name, [R, C], dtype, kind=kind).ap()
        esz = 2 if dtype == BF16 else 4
        return TV("dram", ap.tensor, name, C, 0, R, 0, C * esz, esz)

    def _buckets(self, tv, b):
        if tv.kind == "psum":
            yield (tv.bname, 0)
            return
        for k in range(b[2] // _BUCKET, (b[3] - 1) // _BUCKET + 1):
            yield (tv.bname, k)

    def _dep_waits(self, q, reads, writes, extra=()):
        need = {}
        for tvs, wr in ((reads, False), (writes, True)):
            for tv in tvs:
                b = (0, 128, 0, 1) if tv.kind == "psum" else tv.box()
                rr = tv.kind == "psum"
                for key in self._buckets(tv, b):
                    for (bb, kind, lbl, v) in self.acc.get(key, ()):
                        if (wr or kind == "w" or (rr and lbl != q)) and _overlap(b, bb):
                            if q == "pe" and lbl == "pe":
                                continue
                            if v > need.get(lbl, 0):
                                need[lbl] = v
        for (lbl, v) in extra:
            if v > need.get(lbl, 0):
                need[lbl] = v
        out = []
        hw = self.hwm[q]
        for lbl, v in need.items():
            if hw.get(lbl, 0) >= v:
                continue
            hw[lbl] = v
            out.append((lbl, v))
        return out

    def _note(self, tv, kind, lbl, v):
        b = (0, 128, 0, 1) if tv.kind == "psum" else tv.box()
        for key in self._buckets(tv, b):
            lst = self.acc.setdefault(key, [])
            if kind == "w":
                lst[:] = [r for r in lst if not _covers(b, r[0])]
            else:
                lst[:] = [r for r in lst if not (r[1] == "r" and r[2] == lbl and _covers(b, r[0]))]
            lst.append((b, kind, lbl, v))

    def _record(self, q, fn, reads, writes, ch=None, n_inst=1):
        waits = self._dep_waits(q, reads, writes)
        if ch is None:
            self.count[q] += 1
            lbl, v = q, self.count[q]
            inc = (q, 1)
        else:
            c = self.chan_count.get(ch, 0) + 1
            self.chan_count[ch] = c
            lbl, v = "dma:" + ch, 16 * c
            inc = (lbl, 16)
        self.recs[q].append((waits, fn, inc))
        self.n_inst += n_inst + len(waits)
        for tv in reads:
            self._note(tv, "r", lbl, v)
        for tv in writes:
            self._note(tv, "w", lbl, v)
        return (lbl, v)

    def dma(self, ch, dst, src, q="sync", dst_ap=None, src_ap=None, **kw):
        d = dst_ap if dst_ap is not None else dst.ap()
        s = src_ap if src_ap is not None else src.ap()
        return self._record(q, lambda e: e.dma_start(d, s, **kw), [src], [dst], ch=ch)

    def mm_group(self, psum, ops, psum_ap=None):
        reads, aps = [], []
        for (lt, rt, la, ra) in ops:
            reads += [lt, rt]
            aps.append((la if la is not None else lt.ap(), ra if ra is not None else rt.ap()))
        pap = psum_ap if psum_ap is not None else psum.ap()
        n = len(ops)

        def fn(e):
            inst = None
            for i, (la, ra) in enumerate(aps):
                inst = e.matmul(pap, la, ra, start=(i == 0), stop=(i == n - 1))
            return inst
        return self._record("pe", fn, reads, [psum], n_inst=n)

    def mm(self, psum, lhsT, rhs, start=True, stop=True, lhsT_ap=None, rhs_ap=None, psum_ap=None):
        la = lhsT_ap if lhsT_ap is not None else lhsT.ap()
        ra = rhs_ap if rhs_ap is not None else rhs.ap()
        pap = psum_ap if psum_ap is not None else psum.ap()
        return self._record("pe", lambda e: e.matmul(pap, la, ra, start=start, stop=stop),
                            [lhsT, rhs], [psum])

    def transpose(self, psum, src, ident, psum_ap=None, src_ap=None):
        pap = psum_ap if psum_ap is not None else psum.ap()
        sap = src_ap if src_ap is not None else src.ap()
        return self._record("pe", lambda e: e.transpose(pap, sap, ident.ap()),
                            [src, ident], [psum])

    def tt(self, q, out, a, b, op, a_ap=None, b_ap=None):
        oap, aap = out.ap(), (a_ap if a_ap is not None else a.ap())
        bap = b_ap if b_ap is not None else b.ap()
        return self._record(q, lambda e: e.tensor_tensor(oap, aap, bap, op), [a, b], [out])

    def btt(self, q, out, a, colblk, blk, op, a_ap=None):
        return self.tt(q, out, a, colblk, op, a_ap=a_ap, b_ap=colblk.bcast(blk))

    def ts(self, q, out, a, s1, s2, op0, op1=None, a_ap=None):
        oap, aap = out.ap(), (a_ap if a_ap is not None else a.ap())
        reads = [a]
        s1v, s2v = s1, s2
        if isinstance(s1, TV):
            reads.append(s1); s1v = s1.ap()
        if isinstance(s2, TV):
            reads.append(s2); s2v = s2.ap()
        if op1 is None:
            return self._record(q, lambda e: e.tensor_scalar(oap, aap, s1v, None, op0), reads, [out])
        return self._record(q, lambda e: e.tensor_scalar(oap, aap, s1v, s2v, op0, op1), reads, [out])

    def stt(self, q, out, a, s, b, op0, op1, a_ap=None, b_ap=None):
        oap, aap = out.ap(), (a_ap if a_ap is not None else a.ap())
        bap = b_ap if b_ap is not None else b.ap()
        reads = [a, b]
        sv = s
        if isinstance(s, TV):
            reads.append(s); sv = s.ap()
        return self._record(q, lambda e: e.scalar_tensor_tensor(oap, aap, sv, bap, op0, op1),
                            reads, [out])

    def act(self, q, out, a, func, bias=None, scale=None, a_ap=None, accum_out=None):
        oap, aap = out.ap(), (a_ap if a_ap is not None else a.ap())
        reads, writes = [a], [out]
        kw = {}
        if bias is not None:
            if isinstance(bias, TV):
                reads.append(bias); kw["bias"] = bias.ap()
            else:
                kw["bias"] = bias
        if scale is not None:
            if isinstance(scale, TV):
                reads.append(scale); kw["scale"] = scale.ap()
            else:
                kw["scale"] = scale
        if accum_out is not None:
            writes.append(accum_out); kw["accum_out"] = accum_out.ap()
        return self._record(q, lambda e: e.activation(oap, aap, func, **kw), reads, writes)

    def copy(self, q, out, a, a_ap=None, out_ap=None):
        oap = out_ap if out_ap is not None else out.ap()
        aap = a_ap if a_ap is not None else a.ap()
        if q == "act":
            return self._record(q, lambda e: e.copy(oap, aap), [a], [out])
        return self._record(q, lambda e: e.tensor_copy(oap, aap), [a], [out])

    def reduce(self, q, out, a, op, a_ap=None):
        oap, aap = out.ap(), (a_ap if a_ap is not None else a.ap())
        return self._record(q, lambda e: e.tensor_reduce(oap, aap, AX.X, op), [a], [out])

    def recip(self, q, out, a):
        oap, aap = out.ap(), a.ap()
        return self._record(q, lambda e: e.reciprocal(oap, aap), [a], [out])

    def memset(self, q, out, val, out_ap=None):
        oap = out_ap if out_ap is not None else out.ap()
        return self._record(q, lambda e: e.memset(oap, val), [], [out])

    def raw(self, q, fn, reads, writes, ch=None):
        return self._record(q, fn, reads, writes, ch=ch)

    def chan_point(self, ch):
        return ("dma:" + ch, 16 * self.chan_count.get(ch, 0))

    def build(self, final_deps=()):
        nc = self.nc
        import contextlib
        stack = contextlib.ExitStack()
        sems = {}
        for q in QUEUES:
            sems[q] = stack.enter_context(nc.semaphore(f"s_{q}"))
        for ch in self.chan_count:
            sems["dma:" + ch] = stack.enter_context(nc.semaphore(f"s_{ch}"))
        block = stack.enter_context(nc.Block())

        def mk(qname):
            recs = self.recs[qname]

            def emit(eng):
                for (waits, fn, inc) in recs:
                    for (lbl, v) in waits:
                        eng.wait_ge(sems[lbl], v)
                    inst = fn(eng)
                    inst.then_inc(sems[inc[0]], inc[1])
                if qname == "sync":
                    for (lbl, v) in final_deps:
                        if v > 0 and lbl in sems:
                            eng.wait_ge(sems[lbl], v)
            return emit

        block.tensor(mk("pe"))
        block.scalar(mk("act"))
        block.vector(mk("dve"))
        block.gpsimd(mk("pool"))
        block.sync(mk("sync"))
        stack.close()
        return nc


O_BIG = 0
O_WST = 131072
O_WBF = 147456
O_DTM = 163840
O_SMALL = 167936
O_EVAC = 178176
O_NW = 186368
ARENA_F32 = 50688

B2_HBC = 0
B2_XBC = 24960
B2_STATE = 74112
B2_SBF = 90496
B2_XTM = 92544
B2_DTXS = 100736
B2_SCM = 102784
B2_YTB = 106880
B2_YN = 108928
B2_YT = 117120
B2_LB = 118144
B2_LFM = 120192
B2_HL = 120704
B2_MSK = 122368
B2_YXB = 128512


def build_mamba(L=2048, stop_after=None, skip=(), dbg=False):
    skip = set(skip)
    DM, II, CONV, PROJ = 4096, 4096, 6144, 10304
    NCH = L // 128
    NBLK = L // 256
    TT = min(512, L)
    NT = L // TT

    nc = bass.Bass("TRN2", target_bir_lowering=False)
    p = Prog(nc)

    xt = p.dram("xt", DM, L, F32, kind="ExternalInput")
    win_t = p.dram("win_t", 81 * 128, DM, F32, kind="ExternalInput")
    wout_t = p.dram("wout_t", 32 * 128, DM, F32, kind="ExternalInput")
    wup_t = p.dram("wup_t", 32 * 128, DM, F32, kind="ExternalInput")
    cw_in = p.dram("cw_in", 128, 192, F32, kind="ExternalInput")
    cb_in = p.dram("cb_in", 128, 48, F32, kind="ExternalInput")
    dtb = p.dram("dtb", 1, 64, F32, kind="ExternalInput")
    alog = p.dram("alog", 1, 64, F32, kind="ExternalInput")
    dvec = p.dram("dvec", 1, 64, F32, kind="ExternalInput")
    nw_in = p.dram("nw_in", 128, DM, F32, kind="ExternalInput")
    gwT = p.dram("gwT", DM, 4, F32, kind="ExternalInput")
    ebias = p.dram("ebias", 1, 4, F32, kind="ExternalInput")
    loraA_T = p.dram("loraA_T", DM, 32, F32, kind="ExternalInput")
    loraB_st = p.dram("loraB_st", 32, DM, F32, kind="ExternalInput")
    cst_in = p.dram("cst_in", 128, 512, F32, kind="ExternalInput")

    dk = "ExternalOutput" if dbg else "Internal"
    hbc_sp = p.dram("hbc_sp", CONV, L, BF16, kind=dk)
    gate_sp = p.dram("gate_sp", II, L, BF16, kind=dk)
    yn_sp = p.dram("yn_sp", L, II, BF16, kind=dk)
    attn_sp = p.dram("attn_sp", DM, L, BF16, kind=dk)
    h1f_sp = p.dram("h1f_sp", DM, L, F32, kind=dk)
    ffn_sp = p.dram("ffn_sp", DM, L, BF16, kind=dk)
    out_fm = p.dram("out_fm", DM, L, F32, kind="ExternalOutput")

    AR = p.sbuf("AR", 128, ARENA_F32, F32)
    PS = [p.psum(f"PS{i}", 128, 512, F32) for i in range(7)]
    PB = p.psum("PB", 128, 1024, BF16)
    PSA, PSB, PSC, PSD, PSE, PSF, PSG = PS

    cv = p.carve

    cst = cv(AR, O_SMALL, 512, F32)
    triu = cst[:, 0:128]
    id64 = cst[0:64, 256:320]
    selTf = cst[0:4, 320:352]
    ones128 = cst[:, 352:480]
    onesrow = ones128[0:1, :]
    id128b = cv(AR, O_SMALL + 2048, 128, BF16)
    selTb = cv(AR, O_SMALL + 2304, 32, BF16, 0, 4)
    onescolb = cv(AR, O_SMALL + 2368, 1, BF16)
    negA_rep = cv(AR, O_SMALL + 2432, 64, F32)
    dtb_rep = cv(AR, O_SMALL + 2688, 64, F32)
    D_rep = cv(AR, O_SMALL + 2944, 64, F32)
    eb_rep = cv(AR, O_SMALL + 3200, 4, F32)
    rowA = cv(AR, O_SMALL + 3216, 64, F32, 0, 1)
    lgt = cv(AR, O_SMALL + 3216, 4, F32)
    lgm = cv(AR, O_SMALL + 3232, 1, F32)
    gcol_r = [cv(AR, O_SMALL + 3236, 1, F32), cv(AR, O_SMALL + 3244, 1, F32)]
    gcol2_r = [cv(AR, O_SMALL + 3240, 1, F32), cv(AR, O_SMALL + 3248, 1, F32)]
    ohb = cv(AR, O_SMALL + 3264, 4, BF16)
    ohT = cv(AR, O_SMALL + 3272, 128, BF16, 0, 4)
    id64b = cv(AR, O_SMALL + 9728, 64, BF16, 0, 64)
    rowB = cv(AR, O_SMALL + 3584, 512, F32, 0, 1)
    rowC = cv(AR, O_SMALL + 5632, 512, F32, 0, 1)
    rowD = cv(AR, O_SMALL + 7680, 512, F32, 0, 1)

    p.dma("cst", cst, cst_in)
    p.copy("dve", id128b, cst[:, 128:256])
    p.copy("dve", id64b, id64)
    p.copy("dve", selTb, selTf)
    p.copy("dve", onescolb, cst[:, 479:480])
    for (dst, src, chn, ne) in ((negA_rep, alog, "c1", 64), (dtb_rep, dtb, "c2", 64),
                                (D_rep, dvec, "c3", 64), (eb_rep, ebias, "c4", 4)):
        tmp = rowA[0:1, 0:ne]
        p.dma(chn, tmp, src)
        if dst is negA_rep:
            p.act("act", tmp, tmp, AF.Exp)
            p.ts("dve", tmp, tmp, -1.0, None, ALU.mult)
        p.mm(PSG, cst, tmp, lhsT_ap=onesrow.ap(), rhs_ap=tmp.ap(),
             psum_ap=PSG.handle[0:128, 0:ne])
        p.copy("dve", dst, PSG[:, 0:ne])
    nwv = cv(AR, O_NW, DM, F32)
    p.dma("nw", nwv, nw_in)

    x_fm = cv(AR, O_BIG, 32 * 2048, BF16)
    wst = [cv(AR, O_WST, 2048, F32), cv(AR, O_WST + 8192, 2048, F32)]
    wbf = [cv(AR, O_WBF, 4096, BF16), cv(AR, O_WBF + 8192, 4096, BF16)]
    evac = [cv(AR, O_EVAC, 2048, BF16), cv(AR, O_EVAC + 4096, 2048, BF16)]
    dt_tm = cv(AR, O_DTM, NCH * 64, F32)

    xslots = [wst[0], wst[1], cv(AR, O_WBF, 2048, F32), cv(AR, O_WBF + 8192, 2048, F32)]
    for dc in range(32):
        st = xslots[dc % 4]
        p.dma(f"x{dc % 4}", st[:, 0:L], xt[dc * 128:(dc + 1) * 128, 0:L])
        p.copy(("dve", "pool")[dc % 2], x_fm[:, dc * 2048:dc * 2048 + L], st[:, 0:L])

    castq = {"n": 0}

    def cast_rr(out, a, a_ap=None, out_ap=None):
        q = ("dve", "pool")[castq["n"] % 2]
        castq["n"] += 1
        p.copy(q, out, a, a_ap=a_ap, out_ap=out_ap)

    def load_w_half(dram_w, tile, half, wslot, bfv, ch):
        st = wst[wslot]
        p.dma(ch, st[:, 0:2048], dram_w[tile * 128:(tile + 1) * 128,
                                        half * 2048:(half + 1) * 2048])
        cast_rr(bfv[:, half * 2048:(half + 1) * 2048], st[:, 0:2048])

    st = wst[0]
    src = bass.AP(win_t.handle, 80 * 128 * DM, [[DM, 128], [128, 32], [1, 64]])
    p.dma("w0", st[:, 0:2048], win_t, dst_ap=st.ap3([[st.rowelems, 128], [64, 32], [1, 64]]),
          src_ap=src)
    p.copy("dve", wbf[0][:, 0:2048], st[:, 0:2048])
    for tt in range(NCH):
        p.mm_group(PSE, [(x_fm[:, dc * 2048 + tt * 128: dc * 2048 + tt * 128 + 128],
                          wbf[0][:, dc * 64:(dc + 1) * 64], None, None) for dc in range(32)],
                   psum_ap=PSE.handle[0:128, 0:64])
        p.copy("act", dt_tm[:, tt * 64:(tt + 1) * 64], PSE[:, 0:64])

    def proj_fm(tile_base, n_cc, sp_dram, chp):
        for cc in range(n_cc):
            sl = cc % 2
            bfv = wbf[sl]
            for hf in range(2):
                load_w_half(win_t, tile_base + cc, hf, hf, bfv, f"{chp}{hf}")
            for it in range(NT):
                ps = PS[(cc * NT + it) % 4]
                p.mm_group(ps, [(bfv[:, dc * 128:(dc + 1) * 128],
                                 x_fm[:, dc * 2048 + it * TT: dc * 2048 + it * TT + TT],
                                 None, None) for dc in range(32)],
                           psum_ap=ps.handle[0:128, 0:TT])
                ev = evac[it % 2]
                p.copy("dve", ev[:, 0:TT], ps[:, 0:TT])
                p.dma(f"{chp}s{it % 2}", sp_dram[cc * 128:(cc + 1) * 128, it * TT:(it + 1) * TT],
                      ev[:, 0:TT], q="act")

    proj_fm(32, 48, hbc_sp, "h")
    proj_fm(0, 32, gate_sp, "g")

    if stop_after == "ph1":
        p.build(final_deps=[p.chan_point("hs0"), p.chan_point("hs1"),
                            p.chan_point("gs0"), p.chan_point("gs1")])
        return nc, p

    hbcs = cv(AR, O_BIG + B2_HBC, 48 * 260, BF16)
    xbc2 = [cv(AR, O_BIG + B2_XBC, 48 * 256, BF16),
            cv(AR, O_BIG + B2_XBC + 24576, 48 * 256, BF16)]
    state = cv(AR, O_BIG + B2_STATE, 4096, F32)
    sbf = [cv(AR, O_BIG + B2_SBF, 512, BF16), cv(AR, O_BIG + B2_SBF + 1024, 512, BF16)]
    x_tm = cv(AR, O_BIG + B2_XTM, 4096, BF16)
    dtxs_r = [cv(AR, O_BIG + B2_DTXS, 512, BF16), cv(AR, O_BIG + B2_DTXS + 1024, 512, BF16)]
    yn_tm = cv(AR, O_BIG + B2_YN, 4096, BF16)
    ytmp_r = [cv(AR, O_EVAC + 4096, 512, F32), cv(AR, O_BIG + B2_YTB, 512, F32)]
    yxd_r = [cv(AR, O_EVAC + 6144, 512, F32), cv(AR, O_BIG + B2_YXB, 512, F32)]
    LB = [cv(AR, O_BIG + B2_LB + i * 256, 64, F32) for i in range(8)]
    lb_x, lb_dtp, lb_a, lb_L, lb_eL, lb_eS, lb_dc, lb_ca = LB
    l_fm = cv(AR, O_BIG + B2_LFM, 128, F32)
    hi_fm = cv(AR, O_BIG + B2_HL, 128, BF16, 0, 64)
    lo_fm = cv(AR, O_BIG + B2_HL + 256, 128, BF16, 0, 64)
    hib = cv(AR, O_BIG + B2_HL + 512, 64, BF16)
    lob = cv(AR, O_BIG + B2_HL + 640, 64, BF16)
    Lhl = cv(AR, O_BIG + B2_HL + 768, 64, F32)
    neghi = cv(AR, O_BIG + B2_HL + 1024, 128, BF16, 0, 64)
    neglo = cv(AR, O_BIG + B2_HL + 1280, 128, BF16, 0, 64)
    dm_w = [cv(AR, O_BIG + B2_MSK, 512, F32), cv(AR, O_BIG + B2_MSK + 2048, 512, F32)]
    m_w = [cv(AR, O_BIG + B2_MSK + 4096, 512, BF16), cv(AR, O_BIG + B2_MSK + 5120, 512, BF16)]
    scm_r = [cv(AR, O_BIG + B2_SCM + i * 256, 128, BF16) for i in range(16)]
    scg_r = [cv(AR, O_BIG + B2_YT, 128, BF16), cv(AR, O_BIG + B2_YT + 256, 128, BF16)]
    btm_r = [cv(AR, O_BIG + B2_YT + 512, 128, BF16), cv(AR, O_BIG + B2_YT + 768, 128, BF16)]
    cw_s = cv(AR, O_EVAC, 192, F32)
    cb_s = cv(AR, O_EVAC + 1024, 48, F32)
    cacc = [cv(AR, O_EVAC + 2048, 256, F32), cv(AR, O_EVAC + 3072, 256, F32)]

    p.dma("cw", cw_s, cw_in)
    p.dma("cb", cb_s, cb_in)
    p.memset("pool", state, 0.0)

    def load_hbc(blk):
        t0 = blk * 256
        hs = hbcs
        if blk == 0:
            p.memset("dve", hs, 0.0,
                     out_ap=hs.ap3([[hs.rowelems, 128], [260, 48], [1, 4]]))
            src_ = bass.AP(hbc_sp.handle, 0, [[L, 128], [128 * L, 48], [1, 256]])
            dst = bass.AP(hs.handle, hs.b0 // 2 + 3, [[hs.rowelems, 128], [260, 48], [1, 256]])
            p.dma("hb", hs, hbc_sp, dst_ap=dst, src_ap=src_)
        else:
            src_ = bass.AP(hbc_sp.handle, t0 - 3, [[L, 128], [128 * L, 48], [1, 259]])
            dst = hs.ap3([[hs.rowelems, 128], [260, 48], [1, 259]])
            p.dma("hb", hs, hbc_sp, dst_ap=dst, src_ap=src_)

    def conv_cc(blk, cc):
        if "conv" in skip:
            return
        hs, xbc = hbcs, xbc2[blk % 2]
        ac = cacc[cc % 2]
        for k in range(4):
            sh_ap = bass.AP(hs.handle, hs.b0 // 2 + cc * 260 + k,
                            [[hs.rowelems, 128], [1, 256]])
            if k == 0:
                p.stt("dve", ac, hs, cw_s[:, cc * 4:cc * 4 + 1], cb_s[:, cc:cc + 1],
                      ALU.mult, ALU.add, a_ap=sh_ap, b_ap=cb_s[:, cc:cc + 1].colb(256))
            else:
                p.stt("dve", ac, hs, cw_s[:, cc * 4 + k:cc * 4 + k + 1], ac,
                      ALU.mult, ALU.add, a_ap=sh_ap)
        p.act("act", xbc[:, cc * 256:(cc + 1) * 256], ac, AF.Silu)

    load_hbc(0)
    for cc in range(48):
        conv_cc(0, cc)

    for blk in range(NBLK):
        if blk + 1 < NBLK:
            load_hbc(blk + 1)
        xbc = xbc2[blk % 2]
        for sub in range(2):
            c = blk * 2 + sub
            xo = sub * 128
            dts = dt_tm[:, c * 64:(c + 1) * 64]
            if "lprep" in skip:
                for lb in LB:
                    p.memset("dve", lb, 0.01)
                p.memset("dve", l_fm, 0.01)
                p.memset("dve", Lhl, 0.01)
                p.memset("dve", hi_fm, 0.01)
                p.memset("dve", lo_fm, 0.0)
                p.memset("dve", neghi, -0.01)
                p.memset("dve", neglo, 0.0)
            else:
                p.tt("dve", lb_x, dts, dtb_rep, ALU.add)
                p.act("act", lb_a, lb_x, AF.Abs)
                p.ts("dve", lb_dc, lb_a, -1.0, None, ALU.mult)
                p.act("act", lb_a, lb_dc, AF.Exp)
                p.ts("dve", lb_dc, lb_a, 1.0, None, ALU.add)
                p.act("act", lb_a, lb_dc, AF.Ln)
                p.ts("dve", lb_dtp, lb_x, 0.0, None, ALU.max)
                p.tt("dve", lb_dtp, lb_dtp, lb_a, ALU.add)
                p.tt("dve", lb_a, lb_dtp, negA_rep, ALU.mult)
                p.mm(PSE, triu, lb_a, psum_ap=PSE.handle[0:128, 0:64])
                p.mm(PSE, lb_a, triu, psum_ap=PSE.handle[0:64, 64:192])
                p.mm(PSE, ones128, lb_a, psum_ap=PSE.handle[0:128, 192:256])
                p.copy("dve", lb_L, PSE[:, 0:64])
                p.copy("dve", l_fm[0:64, :], PSE[0:64, 64:192])
                p.copy("dve", lb_x, PSE[:, 192:256])
                p.act("act", lb_eL, lb_L, AF.Exp)
                p.act("act", lb_eS, lb_x, AF.Exp)
                p.tt("dve", lb_dc, lb_x, lb_L, ALU.subtract)
                p.act("act", lb_dc, lb_dc, AF.Exp)
                p.tt("dve", lb_ca, lb_dc, lb_dtp, ALU.mult)
                p.copy("dve", hi_fm, l_fm[0:64, :])
                p.tt("dve", lo_fm, l_fm[0:64, :], hi_fm, ALU.subtract)
                p.ts("dve", neghi, hi_fm, -1.0, None, ALU.mult)
                p.ts("dve", neglo, lo_fm, -1.0, None, ALU.mult)
                p.copy("dve", hib, lb_L)
                p.tt("dve", lob, lb_L, hib, ALU.subtract)
                p.tt("dve", Lhl, hib, lob, ALU.add)

            def a_pe(g):
                if blk + 1 < NBLK:
                    for k3 in range(3):
                        conv_cc(blk + 1, (sub * 8 + g) * 3 + k3)
                if "xtm" not in skip:
                    for i4 in range(4):
                        cc = g * 4 + i4
                        p.transpose(PB, xbc[:, cc * 256 + xo: cc * 256 + xo + 128], id128b,
                                    psum_ap=PB.handle[0:128, i4 * 128:(i4 + 1) * 128])
                else:
                    p.memset("dve", x_tm[:, g * 512:(g + 1) * 512], 0.01)
                B_fm = xbc[:, (32 + g) * 256 + xo:(32 + g) * 256 + xo + 128]
                C_fm = xbc[:, (40 + g) * 256 + xo:(40 + g) * 256 + xo + 128]
                if "scores" not in skip:
                    p.mm(PSE, B_fm, C_fm, psum_ap=PSE.handle[0:128, 256:384])
                else:
                    p.memset("dve", scg_r[g % 2], 0.01)
                if "btm" not in skip:
                    bslot = 4 + g % 2
                    p.transpose(PB, B_fm, id128b,
                                psum_ap=PB.handle[0:128, bslot * 128:(bslot + 1) * 128])
                else:
                    p.memset("dve", btm_r[g % 2], 0.01)
                if "masks" not in skip:
                    for jb in range(2):
                        dps = (PSC, PSD)[jb]
                        for j4 in range(4):
                            h = g * 8 + jb * 4 + j4
                            col = bass.AP(id64b.handle, id64b.b0 // 2 + h,
                                          [[id64b.rowelems, 64], [0, 128]])
                            p.mm_group(dps, [(id64b, hi_fm, col, None),
                                             (neghi, id64b, None, col),
                                             (id64b, lo_fm, col, None),
                                             (neglo, id64b, None, col)],
                                       psum_ap=dps.handle[0:128, j4 * 128:(j4 + 1) * 128])

            def a_elem(g):
                if "xtm" not in skip:
                    for i4 in range(4):
                        cc = g * 4 + i4
                        p.copy("act", x_tm[:, cc * 128:(cc + 1) * 128],
                               PB[:, i4 * 128:(i4 + 1) * 128])
                if "dtxs" not in skip:
                    p.btt("pool", dtxs_r[g % 2], x_tm[:, g * 512:(g + 1) * 512],
                          lb_ca[:, g * 8:(g + 1) * 8], 64, ALU.mult)
                else:
                    p.memset("pool", dtxs_r[g % 2], 0.01)
                scg = scg_r[g % 2]
                if "scores" not in skip:
                    p.tt("dve", scg, PSE[:, 256:384], triu, ALU.mult)
                if "btm" not in skip:
                    bslot = 4 + g % 2
                    p.copy("act", btm_r[g % 2], PB[:, bslot * 128:(bslot + 1) * 128])
                if "scast" not in skip:
                    p.copy("pool", sbf[g % 2], state[:, g * 512:(g + 1) * 512])
                else:
                    p.memset("dve", sbf[g % 2], 0.01)
                if "masks" not in skip:
                    for jb in range(2):
                        dps = (PSC, PSD)[jb]
                        dmw = dm_w[jb]
                        p.ts("dve", dmw, dps[:, 0:512], 0.0, None, ALU.min)
                        mw = m_w[jb]
                        p.act("act", mw, dmw, AF.Exp)
                        for j4 in range(4):
                            j = jb * 4 + j4
                            h = g * 8 + j
                            p.stt("dve", scm_r[(g % 2) * 8 + j], scg,
                                  lb_dtp[:, h:h + 1], mw[:, j4 * 128:(j4 + 1) * 128],
                                  ALU.mult, ALU.mult,
                                  b_ap=mw[:, j4 * 128:(j4 + 1) * 128].ap())

            def b_mm(g):
                C_fm = xbc[:, (40 + g) * 256 + xo:(40 + g) * 256 + xo + 128]
                if "inter" not in skip:
                    p.mm(PSB, C_fm, sbf[g % 2])
                if "intra" not in skip:
                    for j in range(8):
                        h = g * 8 + j
                        scm = scm_r[(g % 2) * 8 + j] if "masks" not in skip else scg_r[g % 2]
                        p.mm(PSA, scm, x_tm[:, h * 64:(h + 1) * 64],
                             psum_ap=PSA.handle[0:128, j * 64:(j + 1) * 64])
                if "supd" not in skip:
                    p.mm(PSF, btm_r[g % 2], dtxs_r[g % 2])

            def b_fast(g):
                ytmp = ytmp_r[g % 2]
                if "ycomb" not in skip:
                    p.btt("dve", ytmp, PSB, lb_eL[:, g * 8:(g + 1) * 8], 64, ALU.mult)
                    p.tt("dve", ytmp, PSA, ytmp, ALU.add)

            def b_slow(g):
                ytmp, yxd = ytmp_r[g % 2], yxd_r[g % 2]
                gcol, gcol2 = gcol_r[g % 2], gcol2_r[g % 2]
                if "ycomb" not in skip:
                    p.btt("pool", yxd, x_tm[:, g * 512:(g + 1) * 512],
                          D_rep[:, g * 8:(g + 1) * 8], 64, ALU.mult)
                    p.tt("pool", yxd, ytmp, yxd, ALU.add)
                if "gn" not in skip:
                    p.act("act", ytmp, yxd, AF.Square, accum_out=gcol)
                    p.ts("pool", gcol2, gcol, 1.0 / 512, 1e-5, ALU.mult, ALU.add)
                    p.act("act", gcol2, gcol2, AF.Ln)
                    p.ts("pool", gcol2, gcol2, -0.5, None, ALU.mult)
                    p.act("act", gcol2, gcol2, AF.Exp)
                    p.tt("pool", ytmp, yxd, gcol2, ALU.mult, b_ap=gcol2.colb(512))
                    p.tt("pool", yn_tm[:, g * 512:(g + 1) * 512], ytmp,
                         nwv[:, g * 512:(g + 1) * 512], ALU.mult)
                else:
                    p.memset("dve", yn_tm[:, g * 512:(g + 1) * 512], 0.0)
                if "supd" not in skip:
                    p.btt("pool", yxd, state[:, g * 512:(g + 1) * 512],
                          lb_eS[:, g * 8:(g + 1) * 8], 64, ALU.mult)
                    p.tt("dve", state[:, g * 512:(g + 1) * 512], yxd, PSF, ALU.add)

            a_pe(0)
            a_elem(0)
            for g in range(8):
                if g + 1 < 8:
                    a_pe(g + 1)
                b_mm(g)
                b_fast(g)
                if g + 1 < 8:
                    a_elem(g + 1)
                b_slow(g)
            p.dma(f"yn{c % 2}", yn_sp[c * 128:(c + 1) * 128, :], yn_tm, q="act")

    if stop_after == "ph2":
        p.build(final_deps=[p.chan_point("yn0"), p.chan_point("yn1")])
        return nc, p

    ygt = cv(AR, O_BIG, 32 * 2048, BF16)
    gtile = [cv(AR, O_NW + 4096 * i, 2048, BF16) for i in range(4)]
    ytile = [cv(AR, O_WST + 4096 * i, 2048, BF16) for i in range(4)]
    for ic in range(32):
        s4 = ic % 4
        gt, yt = gtile[s4], ytile[s4]
        p.dma(f"gg{s4}", gt[:, 0:L], gate_sp[ic * 128:(ic + 1) * 128, :], q="sync")
        p.raw("sync",
              lambda e, d=yt[:, 0:L].ap(), s=yn_sp[0:L, ic * 128:(ic + 1) * 128].ap():
              e.dma_start(d, s, transpose=True), [yn_sp], [yt], ch=f"y{s4}")
        p.act("act", gt[:, 0:L], gt[:, 0:L], AF.Silu)
        p.tt(("dve", "pool")[ic % 2], ygt[:, ic * 2048:ic * 2048 + L], gt[:, 0:L],
             yt[:, 0:L], ALU.mult)

    def big_mm_phase(w_dram, rhs_big, sp_dram, chp, extra_ops=None, ev_slots=None):
        evs = ev_slots if ev_slots is not None else evac
        for oc in range(32):
            sl = oc % 2
            bfv = wbf[sl]
            for hf in range(2):
                load_w_half(w_dram, oc, hf, hf, bfv, f"w{hf}")
            for it in range(NT):
                ps = PS[(oc * NT + it) % 4]
                ops = [(bfv[:, dc * 128:(dc + 1) * 128],
                        rhs_big[:, dc * 2048 + it * TT: dc * 2048 + it * TT + TT], None, None)
                       for dc in range(32)]
                if extra_ops is not None:
                    ops.append(extra_ops(oc, it))
                p.mm_group(ps, ops, psum_ap=ps.handle[0:128, 0:TT])
                ev = evs[it % 2]
                p.copy("dve", ev[:, 0:TT], ps[:, 0:TT])
                p.dma(f"{chp}{it % 2}", sp_dram[oc * 128:(oc + 1) * 128, it * TT:(it + 1) * TT],
                      ev[:, 0:TT], q="act")

    big_mm_phase(wout_t, ygt, attn_sp, "as")

    h1b = cv(AR, O_BIG, 32 * 2048, BF16)
    xstg = [cv(AR, O_WST + i * 2048, 512, F32) for i in range(4)]
    astg = [cv(AR, O_WST + 8192 + i * 1024, 512, BF16) for i in range(4)]
    xres4 = [cv(AR, O_WST + 12288 + i * 2048, 512, F32) for i in range(2)]
    xres4 += [cv(AR, O_WBF + i * 2048, 512, F32) for i in range(2)]
    mrep = cv(AR, O_WBF + 4096, 512, F32)
    rrep = cv(AR, O_WBF + 6144, 512, F32)
    xrbf = [cv(AR, O_WBF + 8192 + i * 1024, 512, BF16) for i in range(4)]

    TT2 = min(512, L)
    NT2 = L // TT2

    def ln_phase(x_dram, add_dram, out_f32_dram, out_bf, tagc, after_tile=None):
        for it in range(NT2):
            t1 = it * TT2
            for icc in range(32):
                s4 = icc % 4
                q = ("dve", "pool")[icc % 2]
                xs, as_, xr = xstg[s4], astg[s4], xres4[s4]
                p.dma(f"lx{s4}", xs[:, 0:TT2], x_dram[icc * 128:(icc + 1) * 128, t1:t1 + TT2])
                p.dma(f"la{s4}", as_[:, 0:TT2],
                      add_dram[icc * 128:(icc + 1) * 128, t1:t1 + TT2])
                p.tt(q, xr[:, 0:TT2], xs[:, 0:TT2], as_[:, 0:TT2], ALU.add)
                p.copy(q, xrbf[s4][:, 0:TT2], xr[:, 0:TT2])
                p.mm(PSE, onescolb, xrbf[s4][:, 0:TT2], start=(icc == 0), stop=(icc == 31),
                     psum_ap=PSE.handle[0:1, 0:TT2])
                p.act("act", as_[:, 0:TT2], xr[:, 0:TT2], AF.Square)
                p.mm(PSF, onescolb, as_[:, 0:TT2], start=(icc == 0), stop=(icc == 31),
                     psum_ap=PSF.handle[0:1, 0:TT2])
            p.ts("dve", rowB[:, 0:TT2], PSE[0:1, 0:TT2], 1.0 / 4096, None, ALU.mult)
            p.ts("dve", rowC[:, 0:TT2], PSF[0:1, 0:TT2], 1.0 / 4096, None, ALU.mult)
            p.tt("dve", rowD[:, 0:TT2], rowB[:, 0:TT2], rowB[:, 0:TT2], ALU.mult)
            p.tt("dve", rowC[:, 0:TT2], rowC[:, 0:TT2], rowD[:, 0:TT2], ALU.subtract)
            p.ts("dve", rowC[:, 0:TT2], rowC[:, 0:TT2], 1e-5, None, ALU.add)
            p.act("act", rowC[:, 0:TT2], rowC[:, 0:TT2], AF.Sqrt)
            p.recip("dve", rowC[:, 0:TT2], rowC[:, 0:TT2])
            p.mm(PSG, cst, rowB, lhsT_ap=onesrow.ap(), rhs_ap=rowB[:, 0:TT2].ap(),
                 psum_ap=PSG.handle[0:128, 0:TT2])
            p.copy("dve", mrep[:, 0:TT2], PSG[:, 0:TT2])
            p.mm(PSG, cst, rowC, lhsT_ap=onesrow.ap(), rhs_ap=rowC[:, 0:TT2].ap(),
                 psum_ap=PSG.handle[0:128, 0:TT2])
            p.copy("dve", rrep[:, 0:TT2], PSG[:, 0:TT2])
            for icc in range(32):
                s4 = icc % 4
                q = ("dve", "pool")[icc % 2]
                xs, as_, xr = xstg[s4], astg[s4], xres4[s4]
                p.dma(f"lx{s4}", xs[:, 0:TT2], x_dram[icc * 128:(icc + 1) * 128, t1:t1 + TT2])
                p.dma(f"la{s4}", as_[:, 0:TT2],
                      add_dram[icc * 128:(icc + 1) * 128, t1:t1 + TT2])
                p.tt(q, xr[:, 0:TT2], xs[:, 0:TT2], as_[:, 0:TT2], ALU.add)
                p.tt(q, xr[:, 0:TT2], xr[:, 0:TT2], mrep[:, 0:TT2], ALU.subtract)
                p.tt(q, xr[:, 0:TT2], xr[:, 0:TT2], rrep[:, 0:TT2], ALU.mult)
                if out_f32_dram is not None:
                    p.dma(f"lo{s4}", out_f32_dram[icc * 128:(icc + 1) * 128, t1:t1 + TT2],
                          xr[:, 0:TT2], q="act")
                    p.copy("act", out_bf[:, icc * 2048 + t1: icc * 2048 + t1 + TT2],
                           xr[:, 0:TT2])
                else:
                    p.dma(f"{tagc}{s4}", out_bf[icc * 128:(icc + 1) * 128, t1:t1 + TT2],
                          xr[:, 0:TT2], q="act")
            if after_tile is not None:
                after_tile(it, t1)

    astT = cv(AR, O_DTM, 1024, BF16)
    wgb = cv(AR, O_DTM + 2048, 128, BF16)
    bstT = cv(AR, O_NW, 4096, BF16, 0, 32)
    wstB = cv(AR, O_WST, 4096, F32, 0, 32)
    zb = cv(AR, O_EVAC, 2048, BF16, 0, 32)
    ohrep = cv(AR, O_EVAC + 4096, 2048, BF16, 0, 32)
    zm = cv(AR, O_NW + 8192, 2048, BF16, 0, 32)
    sm_ev = [cv(AR, O_NW + 12288, 512, BF16), cv(AR, O_NW + 13312, 512, BF16)]

    st0 = wst[0]
    p.dma("w0", st0[:, 0:1024], loraA_T,
          dst_ap=st0.ap3([[st0.rowelems, 128], [32, 32], [1, 32]]),
          src_ap=bass.AP(loraA_T.handle, 0, [[32, 128], [128 * 32, 32], [1, 32]]))
    p.copy("dve", astT, st0[:, 0:1024])
    st1 = wst[1]
    p.dma("w1", st1[:, 0:128], gwT,
          dst_ap=st1.ap3([[st1.rowelems, 128], [4, 32], [1, 4]]),
          src_ap=bass.AP(gwT.handle, 0, [[4, 128], [128 * 4, 32], [1, 4]]))
    p.copy("dve", wgb, st1[:, 0:128])
    p.dma("w0", wstB, loraB_st)
    p.copy("dve", bstT, wstB)

    def ffn_head(it, t1):
        TTl = TT2
        p.mm_group(PSE, [(astT[:, dc * 32:(dc + 1) * 32],
                          h1b[:, dc * 2048 + t1: dc * 2048 + t1 + TTl], None, None)
                         for dc in range(32)],
                   psum_ap=PSE.handle[0:32, 0:TTl])
        p.copy("act", zb[:, t1:t1 + TTl], PSE[0:32, 0:TTl])
        for tts in range(t1 // 128, (t1 + TTl) // 128):
            p.mm_group(PSF, [(h1b[:, dc * 2048 + tts * 128: dc * 2048 + tts * 128 + 128],
                              wgb[:, dc * 4:(dc + 1) * 4], None, None) for dc in range(32)],
                       psum_ap=PSF.handle[0:128, 0:4])
            p.tt("dve", lgt, PSF[:, 0:4], eb_rep, ALU.add)
            p.reduce("dve", lgm, lgt, ALU.max)
            p.ts("dve", ohb, lgt, lgm[:, 0:1], None, ALU.is_ge)
            p.transpose(PB, ohb, id128b, psum_ap=PB.handle[0:4, 0:128])
            p.copy("act", ohT, PB[0:4, 0:128])
            p.mm(PSG, selTb, ohT, psum_ap=PSG.handle[0:32, 0:128])
            p.copy("act", ohrep[:, tts * 128:(tts + 1) * 128], PSG[0:32, 0:128])
            p.tt("dve", zm[:, tts * 128:(tts + 1) * 128], zb[:, tts * 128:(tts + 1) * 128],
                 ohrep[:, tts * 128:(tts + 1) * 128], ALU.mult)

    ln_phase(xt, attn_sp, h1f_sp, h1b, "lo", after_tile=ffn_head)

    def lora_extra(oc, it):
        return (bstT[0:32, oc * 128:(oc + 1) * 128], zm[0:32, it * TT:(it + 1) * TT],
                None, None)

    big_mm_phase(wup_t, h1b, ffn_sp, "fs", extra_ops=lora_extra, ev_slots=sm_ev)

    ln_phase(h1f_sp, ffn_sp, None, out_fm, "o")

    p.build(final_deps=[p.chan_point(f"o{i}") for i in range(4)])
    return nc, p


def tile_w(wT, ntile):
    wT = np.asarray(wT, np.float32)
    out_dim = wT.shape[1]
    pad = ntile * 128 - out_dim
    if pad:
        wT = np.concatenate([wT, np.zeros((wT.shape[0], pad), np.float32)], axis=1)
    r = wT.reshape(32, 128, ntile, 128).transpose(2, 1, 0, 3)
    return np.ascontiguousarray(r.reshape(ntile * 128, 4096))


def make_consts():
    consts = np.zeros((128, 512), np.float32)
    ii, jj = np.arange(128)[:, None], np.arange(128)[None, :]
    consts[:, 0:128] = (ii <= jj)
    consts[:, 128:256] = np.eye(128)
    consts[0:64, 256:320] = np.eye(64)
    for e in range(4):
        consts[e, 320 + 8 * e:320 + 8 * e + 8] = 1.0
    consts[:, 352:480] = 1.0
    return consts


def kernel(**inp):
    a = {k: np.asarray(v, dtype=np.float32) for k, v in inp.items()}
    x = a["L0_attn_hidden_states"]
    B, L, DM = x.shape
    cw = a["L0_attn_conv_weight"]
    cb = a["L0_attn_conv_bias"]
    consts = make_consts()

    common = dict(
        win_t=tile_w(a["L0_attn_in_proj_w"].T, 81),
        wout_t=tile_w(a["L0_attn_out_proj_w"].T, 32),
        wup_t=tile_w(a["L0_ffn_W_up"].T, 32),
        cw_in=np.ascontiguousarray(cw.reshape(48, 128, 4).transpose(1, 0, 2).reshape(128, 192)),
        cb_in=np.ascontiguousarray(cb.reshape(48, 128).T),
        dtb=a["L0_attn_dt_bias"].reshape(1, 64),
        alog=a["L0_attn_A_log"].reshape(1, 64),
        dvec=a["L0_attn_D"].reshape(1, 64),
        nw_in=np.ascontiguousarray(np.broadcast_to(a["L0_attn_norm_weight"], (128, DM))),
        gwT=np.ascontiguousarray(a["L0_ffn_gate_w"].T),
        ebias=a["L0_ffn_expert_bias"].reshape(1, 4),
        loraA_T=np.ascontiguousarray(a["L0_ffn_lora_A"].reshape(32, DM).T),
        loraB_st=np.ascontiguousarray(a["L0_ffn_lora_B"].transpose(0, 2, 1).reshape(32, DM)),
        cst_in=consts,
    )
    nc, _ = build_mamba(L)
    in_maps = [dict(common, xt=np.ascontiguousarray(x[b].T)) for b in range(B)]
    res = run_bass_kernel_spmd(nc, in_maps, core_ids=list(range(B)))
    out = np.stack([np.ascontiguousarray(res.results[b]["out_fm"]).T for b in range(B)])
    return np.ascontiguousarray(out, dtype=np.float32)


# version 10 (final)
# speedup vs baseline: 1.0724x; 1.0371x over previous
import numpy as np
import concourse.bass as bass
import concourse.mybir as mybir
from concourse.bass_utils import run_bass_kernel_spmd

F32 = mybir.dt.float32
BF16 = mybir.dt.bfloat16
AF = mybir.ActivationFunctionType
ALU = mybir.AluOpType
AX = mybir.AxisListType

QUEUES = ("pe", "act", "dve", "pool", "sync")
_BUCKET = 2048


class TV:
    __slots__ = ("kind", "handle", "bname", "rowelems", "p0", "np_", "b0", "nb", "esz")

    def __init__(self, kind, handle, bname, rowelems, p0, np_, b0, nb, esz):
        self.kind, self.handle, self.bname = kind, handle, bname
        self.rowelems = rowelems
        self.p0, self.np_, self.b0, self.nb, self.esz = p0, np_, b0, nb, esz

    @property
    def ne(self):
        return self.nb // self.esz

    def __getitem__(self, idx):
        ps, es = idx
        p0 = ps.start or 0
        p1 = ps.stop if ps.stop is not None else self.np_
        e0 = es.start or 0
        e1 = es.stop if es.stop is not None else self.ne
        assert 0 <= p0 <= p1 <= self.np_ and 0 <= e0 <= e1 <= self.ne, (idx, self.np_, self.ne)
        return TV(self.kind, self.handle, self.bname, self.rowelems,
                  self.p0 + p0, p1 - p0, self.b0 + e0 * self.esz, (e1 - e0) * self.esz, self.esz)

    @property
    def shape(self):
        return (self.np_, self.ne)

    def ap(self):
        if self.kind == "dram":
            return bass.AP(self.handle, self.b0 // self.esz + self.p0 * self.rowelems,
                           [[self.rowelems, self.np_], [1, self.ne]])
        return self.handle[self.p0:self.p0 + self.np_,
                           self.b0 // self.esz: self.b0 // self.esz + self.ne]

    def ap3(self, dims):
        return bass.AP(self.handle, self.b0 // self.esz + self.p0 * self.rowelems, dims)

    def bcast(self, blk):
        return self.ap3([[self.rowelems, self.np_], [1, self.ne], [0, blk]])

    def colb(self, blk):
        assert self.ne == 1
        return self.ap3([[self.rowelems, self.np_], [0, blk]])

    def box(self):
        return (self.p0, self.p0 + self.np_, self.b0, self.b0 + self.nb)


def _overlap(a, b):
    return a[0] < b[1] and b[0] < a[1] and a[2] < b[3] and b[2] < a[3]


def _covers(a, b):
    return a[0] <= b[0] and a[1] >= b[1] and a[2] <= b[2] and a[3] >= b[3]


class Prog:
    def __init__(self, nc):
        self.nc = nc
        self.recs = {q: [] for q in QUEUES}
        self.count = {q: 0 for q in QUEUES}
        self.chan_count = {}
        self.acc = {}
        self.hwm = {q: {} for q in QUEUES}
        self._ctxs = []
        self._bitcache = {}
        self.n_inst = 0

    def sbuf(self, name, P, F, dtype):
        t = self.nc.sbuf_tensor(name, [P, F], dtype)
        h = t.__enter__()
        self._ctxs.append(t)
        esz = 2 if dtype == BF16 else 4
        return TV("sbuf", h, name, F, 0, P, 0, F * esz, esz)

    def carve(self, base, byteoff, nelems, dtype, p0=0, nparts=128):
        esz = 2 if dtype == BF16 else 4
        key = (base.bname, dtype)
        if key not in self._bitcache:
            self._bitcache[key] = base.handle if esz == base.esz else base.handle.bitcast(dtype)
        h = self._bitcache[key]
        rowbytes = base.rowelems * base.esz
        assert byteoff % esz == 0 and byteoff + nelems * esz <= rowbytes, (byteoff, nelems, esz)
        return TV("sbuf", h, base.bname, rowbytes // esz, p0, nparts, byteoff, nelems * esz, esz)

    def psum(self, name, P, F, dtype):
        t = self.nc.psum_tensor(name, [P, F], dtype)
        h = t.__enter__()
        self._ctxs.append(t)
        esz = 2 if dtype == BF16 else 4
        return TV("psum", h, name, F, 0, P, 0, F * esz, esz)

    def dram(self, name, R, C, dtype, kind="Internal"):
        ap = self.nc.dram_tensor(name, [R, C], dtype, kind=kind).ap()
        esz = 2 if dtype == BF16 else 4
        return TV("dram", ap.tensor, name, C, 0, R, 0, C * esz, esz)

    def _buckets(self, tv, b):
        if tv.kind == "psum":
            yield (tv.bname, 0)
            return
        for k in range(b[2] // _BUCKET, (b[3] - 1) // _BUCKET + 1):
            yield (tv.bname, k)

    def _dep_waits(self, q, reads, writes, extra=()):
        need = {}
        for tvs, wr in ((reads, False), (writes, True)):
            for tv in tvs:
                b = (0, 128, 0, 1) if tv.kind == "psum" else tv.box()
                rr = tv.kind == "psum"
                for key in self._buckets(tv, b):
                    for (bb, kind, lbl, v) in self.acc.get(key, ()):
                        if (wr or kind == "w" or (rr and lbl != q)) and _overlap(b, bb):
                            if q == "pe" and lbl == "pe":
                                continue
                            if v > need.get(lbl, 0):
                                need[lbl] = v
        for (lbl, v) in extra:
            if v > need.get(lbl, 0):
                need[lbl] = v
        out = []
        hw = self.hwm[q]
        for lbl, v in need.items():
            if hw.get(lbl, 0) >= v:
                continue
            hw[lbl] = v
            out.append((lbl, v))
        return out

    def _note(self, tv, kind, lbl, v):
        b = (0, 128, 0, 1) if tv.kind == "psum" else tv.box()
        for key in self._buckets(tv, b):
            lst = self.acc.setdefault(key, [])
            if kind == "w":
                lst[:] = [r for r in lst if not _covers(b, r[0])]
            else:
                lst[:] = [r for r in lst if not (r[1] == "r" and r[2] == lbl and _covers(b, r[0]))]
            lst.append((b, kind, lbl, v))

    def _record(self, q, fn, reads, writes, ch=None, n_inst=1):
        waits = self._dep_waits(q, reads, writes)
        if ch is None:
            self.count[q] += 1
            lbl, v = q, self.count[q]
            inc = (q, 1)
        else:
            c = self.chan_count.get(ch, 0) + 1
            self.chan_count[ch] = c
            lbl, v = "dma:" + ch, 16 * c
            inc = (lbl, 16)
        self.recs[q].append((waits, fn, inc))
        self.n_inst += n_inst + len(waits)
        for tv in reads:
            self._note(tv, "r", lbl, v)
        for tv in writes:
            self._note(tv, "w", lbl, v)
        return (lbl, v)

    def dma(self, ch, dst, src, q="sync", dst_ap=None, src_ap=None, **kw):
        d = dst_ap if dst_ap is not None else dst.ap()
        s = src_ap if src_ap is not None else src.ap()
        return self._record(q, lambda e: e.dma_start(d, s, **kw), [src], [dst], ch=ch)

    def mm_group(self, psum, ops, psum_ap=None):
        reads, aps = [], []
        for (lt, rt, la, ra) in ops:
            reads += [lt, rt]
            aps.append((la if la is not None else lt.ap(), ra if ra is not None else rt.ap()))
        pap = psum_ap if psum_ap is not None else psum.ap()
        n = len(ops)

        def fn(e):
            inst = None
            for i, (la, ra) in enumerate(aps):
                inst = e.matmul(pap, la, ra, start=(i == 0), stop=(i == n - 1))
            return inst
        return self._record("pe", fn, reads, [psum], n_inst=n)

    def mm(self, psum, lhsT, rhs, start=True, stop=True, lhsT_ap=None, rhs_ap=None, psum_ap=None):
        la = lhsT_ap if lhsT_ap is not None else lhsT.ap()
        ra = rhs_ap if rhs_ap is not None else rhs.ap()
        pap = psum_ap if psum_ap is not None else psum.ap()
        return self._record("pe", lambda e: e.matmul(pap, la, ra, start=start, stop=stop),
                            [lhsT, rhs], [psum])

    def transpose(self, psum, src, ident, psum_ap=None, src_ap=None):
        pap = psum_ap if psum_ap is not None else psum.ap()
        sap = src_ap if src_ap is not None else src.ap()
        return self._record("pe", lambda e: e.transpose(pap, sap, ident.ap()),
                            [src, ident], [psum])

    def tt(self, q, out, a, b, op, a_ap=None, b_ap=None):
        oap, aap = out.ap(), (a_ap if a_ap is not None else a.ap())
        bap = b_ap if b_ap is not None else b.ap()
        return self._record(q, lambda e: e.tensor_tensor(oap, aap, bap, op), [a, b], [out])

    def btt(self, q, out, a, colblk, blk, op, a_ap=None):
        return self.tt(q, out, a, colblk, op, a_ap=a_ap, b_ap=colblk.bcast(blk))

    def ts(self, q, out, a, s1, s2, op0, op1=None, a_ap=None):
        oap, aap = out.ap(), (a_ap if a_ap is not None else a.ap())
        reads = [a]
        s1v, s2v = s1, s2
        if isinstance(s1, TV):
            reads.append(s1); s1v = s1.ap()
        if isinstance(s2, TV):
            reads.append(s2); s2v = s2.ap()
        if op1 is None:
            return self._record(q, lambda e: e.tensor_scalar(oap, aap, s1v, None, op0), reads, [out])
        return self._record(q, lambda e: e.tensor_scalar(oap, aap, s1v, s2v, op0, op1), reads, [out])

    def stt(self, q, out, a, s, b, op0, op1, a_ap=None, b_ap=None):
        oap, aap = out.ap(), (a_ap if a_ap is not None else a.ap())
        bap = b_ap if b_ap is not None else b.ap()
        reads = [a, b]
        sv = s
        if isinstance(s, TV):
            reads.append(s); sv = s.ap()
        return self._record(q, lambda e: e.scalar_tensor_tensor(oap, aap, sv, bap, op0, op1),
                            reads, [out])

    def act(self, q, out, a, func, bias=None, scale=None, a_ap=None, accum_out=None):
        oap, aap = out.ap(), (a_ap if a_ap is not None else a.ap())
        reads, writes = [a], [out]
        kw = {}
        if bias is not None:
            if isinstance(bias, TV):
                reads.append(bias); kw["bias"] = bias.ap()
            else:
                kw["bias"] = bias
        if scale is not None:
            if isinstance(scale, TV):
                reads.append(scale); kw["scale"] = scale.ap()
            else:
                kw["scale"] = scale
        if accum_out is not None:
            writes.append(accum_out); kw["accum_out"] = accum_out.ap()
        return self._record(q, lambda e: e.activation(oap, aap, func, **kw), reads, writes)

    def copy(self, q, out, a, a_ap=None, out_ap=None):
        oap = out_ap if out_ap is not None else out.ap()
        aap = a_ap if a_ap is not None else a.ap()
        if q == "act":
            return self._record(q, lambda e: e.copy(oap, aap), [a], [out])
        return self._record(q, lambda e: e.tensor_copy(oap, aap), [a], [out])

    def reduce(self, q, out, a, op, a_ap=None):
        oap, aap = out.ap(), (a_ap if a_ap is not None else a.ap())
        return self._record(q, lambda e: e.tensor_reduce(oap, aap, AX.X, op), [a], [out])

    def recip(self, q, out, a):
        oap, aap = out.ap(), a.ap()
        return self._record(q, lambda e: e.reciprocal(oap, aap), [a], [out])

    def memset(self, q, out, val, out_ap=None):
        oap = out_ap if out_ap is not None else out.ap()
        return self._record(q, lambda e: e.memset(oap, val), [], [out])

    def raw(self, q, fn, reads, writes, ch=None):
        return self._record(q, fn, reads, writes, ch=ch)

    def chan_point(self, ch):
        return ("dma:" + ch, 16 * self.chan_count.get(ch, 0))

    def build(self, final_deps=()):
        nc = self.nc
        import contextlib
        stack = contextlib.ExitStack()
        sems = {}
        for q in QUEUES:
            sems[q] = stack.enter_context(nc.semaphore(f"s_{q}"))
        for ch in self.chan_count:
            sems["dma:" + ch] = stack.enter_context(nc.semaphore(f"s_{ch}"))
        block = stack.enter_context(nc.Block())

        def mk(qname):
            recs = self.recs[qname]

            def emit(eng):
                for (waits, fn, inc) in recs:
                    for (lbl, v) in waits:
                        eng.wait_ge(sems[lbl], v)
                    inst = fn(eng)
                    inst.then_inc(sems[inc[0]], inc[1])
                if qname == "sync":
                    for (lbl, v) in final_deps:
                        if v > 0 and lbl in sems:
                            eng.wait_ge(sems[lbl], v)
            return emit

        block.tensor(mk("pe"))
        block.scalar(mk("act"))
        block.vector(mk("dve"))
        block.gpsimd(mk("pool"))
        block.sync(mk("sync"))
        stack.close()
        return nc


O_BIG = 0
O_WST = 131072
O_WBF = 147456
O_DTM = 163840
O_SMALL = 167936
O_EVAC = 178176
O_NW = 186368
ARENA_F32 = 50688

B2_HBC = 0
B2_XBC = 24960
B2_STATE = 74112
B2_SBF = 90496
B2_XTM = 92544
B2_DTXS = 100736
B2_SCM = 102784
B2_YTB = 106880
B2_YN = 108928
B2_YT = 117120
B2_LB = 118144
B2_LFM = 120192
B2_HL = 120704
B2_MSK = 122368
B2_YXB = 128512


def build_mamba(L=2048, stop_after=None, skip=(), dbg=False):
    skip = set(skip)
    DM, II, CONV, PROJ = 4096, 4096, 6144, 10304
    NCH = L // 128
    NBLK = L // 256
    TT = min(512, L)
    NT = L // TT

    nc = bass.Bass("TRN2", target_bir_lowering=False)
    p = Prog(nc)

    xt = p.dram("xt", DM, L, F32, kind="ExternalInput")
    win_t = p.dram("win_t", 81 * 128, DM, F32, kind="ExternalInput")
    wout_t = p.dram("wout_t", 32 * 128, DM, F32, kind="ExternalInput")
    wup_t = p.dram("wup_t", 32 * 128, DM, F32, kind="ExternalInput")
    cw_in = p.dram("cw_in", 128, 192, F32, kind="ExternalInput")
    cb_in = p.dram("cb_in", 128, 48, F32, kind="ExternalInput")
    dtb = p.dram("dtb", 1, 64, F32, kind="ExternalInput")
    alog = p.dram("alog", 1, 64, F32, kind="ExternalInput")
    dvec = p.dram("dvec", 1, 64, F32, kind="ExternalInput")
    nw_in = p.dram("nw_in", 128, DM, F32, kind="ExternalInput")
    gwT = p.dram("gwT", DM, 4, F32, kind="ExternalInput")
    ebias = p.dram("ebias", 1, 4, F32, kind="ExternalInput")
    loraA_T = p.dram("loraA_T", DM, 32, F32, kind="ExternalInput")
    loraB_st = p.dram("loraB_st", 32, DM, F32, kind="ExternalInput")
    cst_in = p.dram("cst_in", 128, 512, F32, kind="ExternalInput")

    dk = "ExternalOutput" if dbg else "Internal"
    hbc_sp = p.dram("hbc_sp", CONV, L, BF16, kind=dk)
    gate_sp = p.dram("gate_sp", II, L, BF16, kind=dk)
    yn_sp = p.dram("yn_sp", L, II, BF16, kind=dk)
    attn_sp = p.dram("attn_sp", DM, L, BF16, kind=dk)
    h1f_sp = p.dram("h1f_sp", DM, L, F32, kind=dk)
    ffn_sp = p.dram("ffn_sp", DM, L, BF16, kind=dk)
    out_fm = p.dram("out_fm", DM, L, F32, kind="ExternalOutput")

    AR = p.sbuf("AR", 128, ARENA_F32, F32)
    PS = [p.psum(f"PS{i}", 128, 512, F32) for i in range(7)]
    PB = p.psum("PB", 128, 1024, BF16)
    PSA, PSB, PSC, PSD, PSE, PSF, PSG = PS

    cv = p.carve

    cst = cv(AR, O_SMALL, 512, F32)
    triu = cst[:, 0:128]
    id64 = cst[0:64, 256:320]
    selTf = cst[0:4, 320:352]
    ones128 = cst[:, 352:480]
    onesrow = ones128[0:1, :]
    id128b = cv(AR, O_SMALL + 2048, 128, BF16)
    selTb = cv(AR, O_SMALL + 2304, 32, BF16, 0, 4)
    onescolb = cv(AR, O_SMALL + 2368, 1, BF16)
    negA_rep = cv(AR, O_SMALL + 2432, 64, F32)
    dtb_rep = cv(AR, O_SMALL + 2688, 64, F32)
    D_rep = cv(AR, O_SMALL + 2944, 64, F32)
    eb_rep = cv(AR, O_SMALL + 3200, 4, F32)
    rowA = cv(AR, O_SMALL + 3216, 64, F32, 0, 1)
    lgt = cv(AR, O_SMALL + 3216, 4, F32)
    lgm = cv(AR, O_SMALL + 3232, 1, F32)
    gcol_r = [cv(AR, O_SMALL + 3236, 1, F32), cv(AR, O_SMALL + 3244, 1, F32)]
    gcol2_r = [cv(AR, O_SMALL + 3240, 1, F32), cv(AR, O_SMALL + 3248, 1, F32)]
    ohb = cv(AR, O_SMALL + 3264, 4, BF16)
    ohT = cv(AR, O_SMALL + 3272, 128, BF16, 0, 4)
    id64b = cv(AR, O_SMALL + 9728, 64, BF16, 0, 64)
    rowB = cv(AR, O_SMALL + 3584, 512, F32, 0, 1)
    rowC = cv(AR, O_SMALL + 5632, 512, F32, 0, 1)
    rowD = cv(AR, O_SMALL + 7680, 512, F32, 0, 1)

    p.dma("cst", cst, cst_in)
    p.copy("dve", id128b, cst[:, 128:256])
    p.copy("dve", id64b, id64)
    p.copy("dve", selTb, selTf)
    p.copy("dve", onescolb, cst[:, 479:480])
    for (dst, src, chn, ne) in ((negA_rep, alog, "c1", 64), (dtb_rep, dtb, "c2", 64),
                                (D_rep, dvec, "c3", 64), (eb_rep, ebias, "c4", 4)):
        tmp = rowA[0:1, 0:ne]
        p.dma(chn, tmp, src)
        if dst is negA_rep:
            p.act("act", tmp, tmp, AF.Exp)
            p.ts("dve", tmp, tmp, -1.0, None, ALU.mult)
        p.mm(PSG, cst, tmp, lhsT_ap=onesrow.ap(), rhs_ap=tmp.ap(),
             psum_ap=PSG.handle[0:128, 0:ne])
        p.copy("dve", dst, PSG[:, 0:ne])
    nwv = cv(AR, O_NW, DM, F32)
    p.dma("nw", nwv, nw_in)

    x_fm = cv(AR, O_BIG, 32 * 2048, BF16)
    wst = [cv(AR, O_WST, 2048, F32), cv(AR, O_WST + 8192, 2048, F32)]
    wbf = [cv(AR, O_WBF, 4096, BF16), cv(AR, O_WBF + 8192, 4096, BF16)]
    evac = [cv(AR, O_EVAC, 2048, BF16), cv(AR, O_EVAC + 4096, 2048, BF16)]
    dt_tm = cv(AR, O_DTM, NCH * 64, F32)

    xslots = [wst[0], wst[1], cv(AR, O_WBF, 2048, F32), cv(AR, O_WBF + 8192, 2048, F32)]
    for dc in range(32):
        st = xslots[dc % 4]
        p.dma(f"x{dc % 4}", st[:, 0:L], xt[dc * 128:(dc + 1) * 128, 0:L])
        p.copy(("dve", "pool")[dc % 2], x_fm[:, dc * 2048:dc * 2048 + L], st[:, 0:L])

    castq = {"n": 0}

    def cast_rr(out, a, a_ap=None, out_ap=None):
        q = ("dve", "pool")[castq["n"] % 2]
        castq["n"] += 1
        p.copy(q, out, a, a_ap=a_ap, out_ap=out_ap)

    def load_w_half(dram_w, tile, half, wslot, bfv, ch):
        st = wst[wslot]
        p.dma(ch, st[:, 0:2048], dram_w[tile * 128:(tile + 1) * 128,
                                        half * 2048:(half + 1) * 2048])
        cast_rr(bfv[:, half * 2048:(half + 1) * 2048], st[:, 0:2048])

    st = wst[0]
    src = bass.AP(win_t.handle, 80 * 128 * DM, [[DM, 128], [128, 32], [1, 64]])
    p.dma("w0", st[:, 0:2048], win_t, dst_ap=st.ap3([[st.rowelems, 128], [64, 32], [1, 64]]),
          src_ap=src)
    p.copy("dve", wbf[0][:, 0:2048], st[:, 0:2048])
    for tt in range(NCH):
        p.mm_group(PSE, [(x_fm[:, dc * 2048 + tt * 128: dc * 2048 + tt * 128 + 128],
                          wbf[0][:, dc * 64:(dc + 1) * 64], None, None) for dc in range(32)],
                   psum_ap=PSE.handle[0:128, 0:64])
        p.copy("act", dt_tm[:, tt * 64:(tt + 1) * 64], PSE[:, 0:64])

    def proj_fm(tile_base, n_cc, sp_dram, chp):
        for cc in range(n_cc):
            sl = cc % 2
            bfv = wbf[sl]
            for hf in range(2):
                load_w_half(win_t, tile_base + cc, hf, hf, bfv, f"{chp}{hf}")
            for it in range(NT):
                ps = PS[(cc * NT + it) % 4]
                p.mm_group(ps, [(bfv[:, dc * 128:(dc + 1) * 128],
                                 x_fm[:, dc * 2048 + it * TT: dc * 2048 + it * TT + TT],
                                 None, None) for dc in range(32)],
                           psum_ap=ps.handle[0:128, 0:TT])
                ev = evac[it % 2]
                p.copy("dve", ev[:, 0:TT], ps[:, 0:TT])
                p.dma(f"{chp}s{it % 2}", sp_dram[cc * 128:(cc + 1) * 128, it * TT:(it + 1) * TT],
                      ev[:, 0:TT], q="act")

    proj_fm(32, 48, hbc_sp, "h")
    proj_fm(0, 32, gate_sp, "g")

    if stop_after == "ph1":
        p.build(final_deps=[p.chan_point("hs0"), p.chan_point("hs1"),
                            p.chan_point("gs0"), p.chan_point("gs1")])
        return nc, p

    hbcs = cv(AR, O_BIG + B2_HBC, 48 * 260, BF16)
    xbc2 = [cv(AR, O_BIG + B2_XBC, 48 * 256, BF16),
            cv(AR, O_BIG + B2_XBC + 24576, 48 * 256, BF16)]
    state = cv(AR, O_BIG + B2_STATE, 4096, F32)
    sbf = [cv(AR, O_BIG + B2_SBF, 512, BF16), cv(AR, O_BIG + B2_SBF + 1024, 512, BF16)]
    x_tm = cv(AR, O_BIG + B2_XTM, 4096, BF16)
    dtxs_r = [cv(AR, O_BIG + B2_DTXS, 512, BF16), cv(AR, O_BIG + B2_DTXS + 1024, 512, BF16)]
    yn_tm = cv(AR, O_BIG + B2_YN, 4096, BF16)
    ytmp_r = [cv(AR, O_EVAC + 4096, 512, F32), cv(AR, O_BIG + B2_YTB, 512, F32)]
    yxd_r = [cv(AR, O_EVAC + 6144, 512, F32), cv(AR, O_BIG + B2_YXB, 512, F32)]
    LB = [cv(AR, O_BIG + B2_LB + i * 256, 64, F32) for i in range(8)]
    lb_x, lb_dtp, lb_a, lb_L, lb_eL, lb_eS, lb_dc, lb_ca = LB
    l_fm = cv(AR, O_BIG + B2_LFM, 128, F32)
    hi_fm = cv(AR, O_BIG + B2_HL, 128, BF16, 0, 64)
    lo_fm = cv(AR, O_BIG + B2_HL + 256, 128, BF16, 0, 64)
    hib = cv(AR, O_BIG + B2_HL + 512, 64, BF16)
    lob = cv(AR, O_BIG + B2_HL + 640, 64, BF16)
    Lhl = cv(AR, O_BIG + B2_HL + 768, 64, F32)
    neghi = cv(AR, O_BIG + B2_HL + 1024, 128, BF16, 0, 64)
    neglo = cv(AR, O_BIG + B2_HL + 1280, 128, BF16, 0, 64)
    dm_w = [cv(AR, O_BIG + B2_MSK, 512, F32), cv(AR, O_BIG + B2_MSK + 2048, 512, F32)]
    m_w = [cv(AR, O_BIG + B2_MSK + 4096, 512, BF16), cv(AR, O_BIG + B2_MSK + 5120, 512, BF16)]
    scm_r = [cv(AR, O_BIG + B2_SCM + i * 256, 128, BF16) for i in range(16)]
    scg_r = [cv(AR, O_BIG + B2_YT, 128, BF16), cv(AR, O_BIG + B2_YT + 256, 128, BF16)]
    btm_r = [cv(AR, O_BIG + B2_YT + 512, 128, BF16), cv(AR, O_BIG + B2_YT + 768, 128, BF16)]
    cw_s = cv(AR, O_EVAC, 192, F32)
    cb_s = cv(AR, O_EVAC + 1024, 48, F32)
    cacc = [cv(AR, O_EVAC + 2048, 256, F32), cv(AR, O_EVAC + 3072, 256, F32)]

    p.dma("cw", cw_s, cw_in)
    p.dma("cb", cb_s, cb_in)
    p.memset("pool", state, 0.0)

    def load_hbc(blk):
        t0 = blk * 256
        hs = hbcs
        if blk == 0:
            p.memset("dve", hs, 0.0,
                     out_ap=hs.ap3([[hs.rowelems, 128], [260, 48], [1, 4]]))
            src_ = bass.AP(hbc_sp.handle, 0, [[L, 128], [128 * L, 48], [1, 256]])
            dst = bass.AP(hs.handle, hs.b0 // 2 + 3, [[hs.rowelems, 128], [260, 48], [1, 256]])
            p.dma("hb", hs, hbc_sp, dst_ap=dst, src_ap=src_)
        else:
            src_ = bass.AP(hbc_sp.handle, t0 - 3, [[L, 128], [128 * L, 48], [1, 259]])
            dst = hs.ap3([[hs.rowelems, 128], [260, 48], [1, 259]])
            p.dma("hb", hs, hbc_sp, dst_ap=dst, src_ap=src_)

    def conv_cc(blk, cc):
        if "conv" in skip:
            return
        hs, xbc = hbcs, xbc2[blk % 2]
        ac = cacc[cc % 2]
        for k in range(4):
            sh_ap = bass.AP(hs.handle, hs.b0 // 2 + cc * 260 + k,
                            [[hs.rowelems, 128], [1, 256]])
            if k == 0:
                p.stt("dve", ac, hs, cw_s[:, cc * 4:cc * 4 + 1], cb_s[:, cc:cc + 1],
                      ALU.mult, ALU.add, a_ap=sh_ap, b_ap=cb_s[:, cc:cc + 1].colb(256))
            else:
                p.stt("dve", ac, hs, cw_s[:, cc * 4 + k:cc * 4 + k + 1], ac,
                      ALU.mult, ALU.add, a_ap=sh_ap)
        p.act("act", xbc[:, cc * 256:(cc + 1) * 256], ac, AF.Silu)

    load_hbc(0)
    for cc in range(48):
        conv_cc(0, cc)

    for blk in range(NBLK):
        if blk + 1 < NBLK:
            load_hbc(blk + 1)
        xbc = xbc2[blk % 2]
        for sub in range(2):
            c = blk * 2 + sub
            xo = sub * 128
            dts = dt_tm[:, c * 64:(c + 1) * 64]
            if "lprep" in skip:
                for lb in LB:
                    p.memset("dve", lb, 0.01)
                p.memset("dve", l_fm, 0.01)
                p.memset("dve", Lhl, 0.01)
                p.memset("dve", hi_fm, 0.01)
                p.memset("dve", lo_fm, 0.0)
                p.memset("dve", neghi, -0.01)
                p.memset("dve", neglo, 0.0)
            else:
                p.tt("dve", lb_x, dts, dtb_rep, ALU.add)
                p.act("act", lb_a, lb_x, AF.Abs)
                p.ts("dve", lb_dc, lb_a, -1.0, None, ALU.mult)
                p.act("act", lb_a, lb_dc, AF.Exp)
                p.ts("dve", lb_dc, lb_a, 1.0, None, ALU.add)
                p.act("act", lb_a, lb_dc, AF.Ln)
                p.ts("dve", lb_dtp, lb_x, 0.0, None, ALU.max)
                p.tt("dve", lb_dtp, lb_dtp, lb_a, ALU.add)
                p.tt("dve", lb_a, lb_dtp, negA_rep, ALU.mult)
                p.mm(PSE, triu, lb_a, psum_ap=PSE.handle[0:128, 0:64])
                p.mm(PSE, lb_a, triu, psum_ap=PSE.handle[0:64, 64:192])
                p.mm(PSE, ones128, lb_a, psum_ap=PSE.handle[0:128, 192:256])
                p.copy("dve", lb_L, PSE[:, 0:64])
                p.copy("dve", l_fm[0:64, :], PSE[0:64, 64:192])
                p.copy("dve", lb_x, PSE[:, 192:256])
                p.act("act", lb_eL, lb_L, AF.Exp)
                p.act("act", lb_eS, lb_x, AF.Exp)
                p.tt("dve", lb_dc, lb_x, lb_L, ALU.subtract)
                p.act("act", lb_dc, lb_dc, AF.Exp)
                p.tt("dve", lb_ca, lb_dc, lb_dtp, ALU.mult)
                p.copy("dve", hi_fm, l_fm[0:64, :])
                p.tt("dve", lo_fm, l_fm[0:64, :], hi_fm, ALU.subtract)
                p.ts("dve", neghi, hi_fm, -1.0, None, ALU.mult)
                p.ts("dve", neglo, lo_fm, -1.0, None, ALU.mult)
                p.copy("dve", hib, lb_L)
                p.tt("dve", lob, lb_L, hib, ALU.subtract)
                p.tt("dve", Lhl, hib, lob, ALU.add)

            def a_pe(g):
                if blk + 1 < NBLK:
                    for k3 in range(3):
                        conv_cc(blk + 1, (sub * 8 + g) * 3 + k3)
                if "xtm" not in skip:
                    for i4 in range(4):
                        cc = g * 4 + i4
                        p.transpose(PB, xbc[:, cc * 256 + xo: cc * 256 + xo + 128], id128b,
                                    psum_ap=PB.handle[0:128, i4 * 128:(i4 + 1) * 128])
                else:
                    p.memset("dve", x_tm[:, g * 512:(g + 1) * 512], 0.01)
                B_fm = xbc[:, (32 + g) * 256 + xo:(32 + g) * 256 + xo + 128]
                C_fm = xbc[:, (40 + g) * 256 + xo:(40 + g) * 256 + xo + 128]
                if "scores" not in skip:
                    p.mm(PSE, B_fm, C_fm, psum_ap=PSE.handle[0:128, 256:384])
                else:
                    p.memset("dve", scg_r[g % 2], 0.01)
                if "btm" not in skip:
                    bslot = 4 + g % 2
                    p.transpose(PB, B_fm, id128b,
                                psum_ap=PB.handle[0:128, bslot * 128:(bslot + 1) * 128])
                else:
                    p.memset("dve", btm_r[g % 2], 0.01)
                if "masks" not in skip:
                    for jb in range(2):
                        dps = (PSC, PSD)[jb]
                        for j4 in range(4):
                            h = g * 8 + jb * 4 + j4
                            col = bass.AP(id64b.handle, id64b.b0 // 2 + h,
                                          [[id64b.rowelems, 64], [0, 128]])
                            p.mm_group(dps, [(id64b, hi_fm, col, None),
                                             (neghi, id64b, None, col),
                                             (id64b, lo_fm, col, None),
                                             (neglo, id64b, None, col)],
                                       psum_ap=dps.handle[0:128, j4 * 128:(j4 + 1) * 128])

            def a_elem(g):
                if "xtm" not in skip:
                    for i4 in range(4):
                        cc = g * 4 + i4
                        p.copy("act", x_tm[:, cc * 128:(cc + 1) * 128],
                               PB[:, i4 * 128:(i4 + 1) * 128])
                if "dtxs" not in skip:
                    p.btt("pool", dtxs_r[g % 2], x_tm[:, g * 512:(g + 1) * 512],
                          lb_ca[:, g * 8:(g + 1) * 8], 64, ALU.mult)
                else:
                    p.memset("pool", dtxs_r[g % 2], 0.01)
                scg = scg_r[g % 2]
                if "scores" not in skip:
                    p.tt("dve", scg, PSE[:, 256:384], triu, ALU.mult)
                if "btm" not in skip:
                    bslot = 4 + g % 2
                    p.copy("act", btm_r[g % 2], PB[:, bslot * 128:(bslot + 1) * 128])
                if "scast" not in skip:
                    p.copy("pool", sbf[g % 2], state[:, g * 512:(g + 1) * 512])
                else:
                    p.memset("dve", sbf[g % 2], 0.01)
                if "masks" not in skip:
                    for jb in range(2):
                        dps = (PSC, PSD)[jb]
                        dmw = dm_w[jb]
                        p.ts("dve", dmw, dps[:, 0:512], 0.0, None, ALU.min)
                        mw = m_w[jb]
                        p.act("act", mw, dmw, AF.Exp)
                        for j4 in range(4):
                            j = jb * 4 + j4
                            h = g * 8 + j
                            p.stt("dve", scm_r[(g % 2) * 8 + j], scg,
                                  lb_dtp[:, h:h + 1], mw[:, j4 * 128:(j4 + 1) * 128],
                                  ALU.mult, ALU.mult,
                                  b_ap=mw[:, j4 * 128:(j4 + 1) * 128].ap())

            def b_mm(g):
                C_fm = xbc[:, (40 + g) * 256 + xo:(40 + g) * 256 + xo + 128]
                if "inter" not in skip:
                    p.mm(PSB, C_fm, sbf[g % 2])
                if "intra" not in skip:
                    for j in range(8):
                        h = g * 8 + j
                        scm = scm_r[(g % 2) * 8 + j] if "masks" not in skip else scg_r[g % 2]
                        p.mm(PSA, scm, x_tm[:, h * 64:(h + 1) * 64],
                             psum_ap=PSA.handle[0:128, j * 64:(j + 1) * 64])
                if "supd" not in skip:
                    p.mm(PSF, btm_r[g % 2], dtxs_r[g % 2])

            def b_fast(g):
                ytmp = ytmp_r[g % 2]
                if "ycomb" not in skip:
                    p.btt("dve", ytmp, PSB, lb_eL[:, g * 8:(g + 1) * 8], 64, ALU.mult)
                    p.tt("dve", ytmp, PSA, ytmp, ALU.add)

            def b_slow(g):
                ytmp, yxd = ytmp_r[g % 2], yxd_r[g % 2]
                gcol, gcol2 = gcol_r[g % 2], gcol2_r[g % 2]
                if "ycomb" not in skip:
                    p.btt("pool", yxd, x_tm[:, g * 512:(g + 1) * 512],
                          D_rep[:, g * 8:(g + 1) * 8], 64, ALU.mult)
                    p.tt("pool", yxd, ytmp, yxd, ALU.add)
                if "gn" not in skip:
                    p.act("act", ytmp, yxd, AF.Square, accum_out=gcol)
                    p.ts("pool", gcol2, gcol, 1.0 / 512, 1e-5, ALU.mult, ALU.add)
                    p.act("act", gcol2, gcol2, AF.Ln)
                    p.ts("pool", gcol2, gcol2, -0.5, None, ALU.mult)
                    p.act("act", gcol2, gcol2, AF.Exp)
                    p.tt("pool", ytmp, yxd, gcol2, ALU.mult, b_ap=gcol2.colb(512))
                    p.tt("pool", yn_tm[:, g * 512:(g + 1) * 512], ytmp,
                         nwv[:, g * 512:(g + 1) * 512], ALU.mult)
                else:
                    p.memset("dve", yn_tm[:, g * 512:(g + 1) * 512], 0.0)
                if "supd" not in skip:
                    p.btt("pool", yxd, state[:, g * 512:(g + 1) * 512],
                          lb_eS[:, g * 8:(g + 1) * 8], 64, ALU.mult)
                    p.tt("dve", state[:, g * 512:(g + 1) * 512], yxd, PSF, ALU.add)

            a_pe(0)
            a_elem(0)
            for g in range(8):
                if g + 1 < 8:
                    a_pe(g + 1)
                b_mm(g)
                b_fast(g)
                if g + 1 < 8:
                    a_elem(g + 1)
                b_slow(g)
            p.dma(f"yn{c % 2}", yn_sp[c * 128:(c + 1) * 128, :], yn_tm, q="act")

    if stop_after == "ph2":
        p.build(final_deps=[p.chan_point("yn0"), p.chan_point("yn1")])
        return nc, p

    ygt = cv(AR, O_BIG, 32 * 2048, BF16)
    gtile = [cv(AR, O_NW + 4096 * i, 2048, BF16) for i in range(4)]
    ytile = [cv(AR, O_WST + 4096 * i, 2048, BF16) for i in range(4)]
    sgt = [cv(AR, O_EVAC + 4096 * i, 2048, BF16) for i in range(2)]
    for ic in range(32):
        s4 = ic % 4
        gt, yt = gtile[s4], ytile[s4]
        p.dma(f"gg{s4}", gt[:, 0:L], gate_sp[ic * 128:(ic + 1) * 128, :], q="sync")
        p.raw("sync",
              lambda e, d=yt[:, 0:L].ap(), s=yn_sp[0:L, ic * 128:(ic + 1) * 128].ap():
              e.dma_start(d, s, transpose=True), [yn_sp], [yt], ch=f"y{s4}")
        sg = sgt[ic % 2]
        p.act("act", sg[:, 0:L], gt[:, 0:L], AF.Silu)
        p.tt(("dve", "pool")[ic % 2], ygt[:, ic * 2048:ic * 2048 + L], sg[:, 0:L],
             yt[:, 0:L], ALU.mult)

    def big_mm_phase(w_dram, rhs_big, sp_dram, chp, extra_ops=None, ev_slots=None):
        evs = ev_slots if ev_slots is not None else evac
        for oc in range(32):
            sl = oc % 2
            bfv = wbf[sl]
            for hf in range(2):
                load_w_half(w_dram, oc, hf, hf, bfv, f"w{hf}")
            for it in range(NT):
                ps = PS[(oc * NT + it) % 4]
                ops = [(bfv[:, dc * 128:(dc + 1) * 128],
                        rhs_big[:, dc * 2048 + it * TT: dc * 2048 + it * TT + TT], None, None)
                       for dc in range(32)]
                if extra_ops is not None:
                    ops.append(extra_ops(oc, it))
                p.mm_group(ps, ops, psum_ap=ps.handle[0:128, 0:TT])
                ev = evs[it % 2]
                p.copy("dve", ev[:, 0:TT], ps[:, 0:TT])
                p.dma(f"{chp}{it % 2}", sp_dram[oc * 128:(oc + 1) * 128, it * TT:(it + 1) * TT],
                      ev[:, 0:TT], q="act")

    big_mm_phase(wout_t, ygt, attn_sp, "as")

    h1b = cv(AR, O_BIG, 32 * 2048, BF16)
    xstg = [cv(AR, O_WST + i * 2048, 512, F32) for i in range(4)]
    astg = [cv(AR, O_WST + 8192 + i * 1024, 512, BF16) for i in range(4)]
    xrbf = [cv(AR, O_WST + 12288 + i * 1024, 512, BF16) for i in range(4)]
    xres4 = [cv(AR, O_WBF + i * 2048, 512, F32) for i in range(4)]
    xresB = [cv(AR, O_WBF + 8192 + i * 2048, 512, F32) for i in range(2)]
    mrep = cv(AR, O_WBF + 12288, 512, F32)
    rrep = cv(AR, O_WBF + 14336, 512, F32)

    TT2 = min(512, L)
    NT2 = L // TT2

    def ln_phase(x_dram, add_dram, out_f32_dram, out_bf, tagc, after_tile=None):
        for it in range(NT2):
            t1 = it * TT2
            for icc in range(32):
                s4 = icc % 4
                q = "pool" if icc % 4 == 3 else "dve"
                xs, as_, xr = xstg[s4], astg[s4], xres4[s4]
                p.dma(f"lx{s4}", xs[:, 0:TT2], x_dram[icc * 128:(icc + 1) * 128, t1:t1 + TT2])
                p.dma(f"la{s4}", as_[:, 0:TT2],
                      add_dram[icc * 128:(icc + 1) * 128, t1:t1 + TT2])
                p.tt(q, xr[:, 0:TT2], xs[:, 0:TT2], as_[:, 0:TT2], ALU.add)
                p.copy(q, xrbf[s4][:, 0:TT2], xr[:, 0:TT2])
                p.mm(PSE, onescolb, xrbf[s4][:, 0:TT2], start=(icc == 0), stop=(icc == 31),
                     psum_ap=PSE.handle[0:1, 0:TT2])
                p.act("act", as_[:, 0:TT2], xr[:, 0:TT2], AF.Square)
                p.mm(PSF, onescolb, as_[:, 0:TT2], start=(icc == 0), stop=(icc == 31),
                     psum_ap=PSF.handle[0:1, 0:TT2])
            p.ts("dve", rowB[:, 0:TT2], PSE[0:1, 0:TT2], 1.0 / 4096, None, ALU.mult)
            p.ts("dve", rowC[:, 0:TT2], PSF[0:1, 0:TT2], 1.0 / 4096, None, ALU.mult)
            p.tt("dve", rowD[:, 0:TT2], rowB[:, 0:TT2], rowB[:, 0:TT2], ALU.mult)
            p.tt("dve", rowC[:, 0:TT2], rowC[:, 0:TT2], rowD[:, 0:TT2], ALU.subtract)
            p.ts("dve", rowC[:, 0:TT2], rowC[:, 0:TT2], 1e-5, None, ALU.add)
            p.act("act", rowC[:, 0:TT2], rowC[:, 0:TT2], AF.Sqrt)
            p.recip("dve", rowC[:, 0:TT2], rowC[:, 0:TT2])
            p.mm(PSG, cst, rowB, lhsT_ap=onesrow.ap(), rhs_ap=rowB[:, 0:TT2].ap(),
                 psum_ap=PSG.handle[0:128, 0:TT2])
            p.copy("dve", mrep[:, 0:TT2], PSG[:, 0:TT2])
            p.mm(PSG, cst, rowC, lhsT_ap=onesrow.ap(), rhs_ap=rowC[:, 0:TT2].ap(),
                 psum_ap=PSG.handle[0:128, 0:TT2])
            p.copy("dve", rrep[:, 0:TT2], PSG[:, 0:TT2])
            for icc in range(32):
                s4 = icc % 4
                q = "pool" if icc % 4 == 3 else "dve"
                xs, as_, xr = xstg[s4], astg[s4], xres4[s4]
                xrb = xresB[icc % 2]
                p.dma(f"lx{s4}", xs[:, 0:TT2], x_dram[icc * 128:(icc + 1) * 128, t1:t1 + TT2])
                p.dma(f"la{s4}", as_[:, 0:TT2],
                      add_dram[icc * 128:(icc + 1) * 128, t1:t1 + TT2])
                p.tt(q, xr[:, 0:TT2], xs[:, 0:TT2], as_[:, 0:TT2], ALU.add)
                p.tt(q, xrb[:, 0:TT2], xr[:, 0:TT2], mrep[:, 0:TT2], ALU.subtract)
                p.tt(q, xr[:, 0:TT2], xrb[:, 0:TT2], rrep[:, 0:TT2], ALU.mult)
                if out_f32_dram is not None:
                    p.dma(f"lo{s4}", out_f32_dram[icc * 128:(icc + 1) * 128, t1:t1 + TT2],
                          xr[:, 0:TT2], q="act")
                    p.copy("act", out_bf[:, icc * 2048 + t1: icc * 2048 + t1 + TT2],
                           xr[:, 0:TT2])
                else:
                    p.dma(f"{tagc}{s4}", out_bf[icc * 128:(icc + 1) * 128, t1:t1 + TT2],
                          xr[:, 0:TT2], q="act")
            if after_tile is not None:
                after_tile(it, t1)

    astT = cv(AR, O_DTM, 1024, BF16)
    wgb = cv(AR, O_DTM + 2048, 128, BF16)
    bstT = cv(AR, O_NW, 4096, BF16, 0, 32)
    wstB = cv(AR, O_WST, 4096, F32, 0, 32)
    zb = cv(AR, O_EVAC, 2048, BF16, 0, 32)
    ohrep = cv(AR, O_EVAC + 4096, 2048, BF16, 0, 32)
    zm = cv(AR, O_NW + 8192, 2048, BF16, 0, 32)
    sm_ev = [cv(AR, O_NW + 12288, 512, BF16), cv(AR, O_NW + 13312, 512, BF16)]

    st0 = wst[0]
    p.dma("w0", st0[:, 0:1024], loraA_T,
          dst_ap=st0.ap3([[st0.rowelems, 128], [32, 32], [1, 32]]),
          src_ap=bass.AP(loraA_T.handle, 0, [[32, 128], [128 * 32, 32], [1, 32]]))
    p.copy("dve", astT, st0[:, 0:1024])
    st1 = wst[1]
    p.dma("w1", st1[:, 0:128], gwT,
          dst_ap=st1.ap3([[st1.rowelems, 128], [4, 32], [1, 4]]),
          src_ap=bass.AP(gwT.handle, 0, [[4, 128], [128 * 4, 32], [1, 4]]))
    p.copy("dve", wgb, st1[:, 0:128])
    p.dma("w0", wstB, loraB_st)
    p.copy("dve", bstT, wstB)

    def ffn_head(it, t1):
        TTl = TT2
        p.mm_group(PSE, [(astT[:, dc * 32:(dc + 1) * 32],
                          h1b[:, dc * 2048 + t1: dc * 2048 + t1 + TTl], None, None)
                         for dc in range(32)],
                   psum_ap=PSE.handle[0:32, 0:TTl])
        p.copy("act", zb[:, t1:t1 + TTl], PSE[0:32, 0:TTl])
        for tts in range(t1 // 128, (t1 + TTl) // 128):
            p.mm_group(PSF, [(h1b[:, dc * 2048 + tts * 128: dc * 2048 + tts * 128 + 128],
                              wgb[:, dc * 4:(dc + 1) * 4], None, None) for dc in range(32)],
                       psum_ap=PSF.handle[0:128, 0:4])
            p.tt("dve", lgt, PSF[:, 0:4], eb_rep, ALU.add)
            p.reduce("dve", lgm, lgt, ALU.max)
            p.ts("dve", ohb, lgt, lgm[:, 0:1], None, ALU.is_ge)
            p.transpose(PB, ohb, id128b, psum_ap=PB.handle[0:4, 0:128])
            p.copy("act", ohT, PB[0:4, 0:128])
            p.mm(PSG, selTb, ohT, psum_ap=PSG.handle[0:32, 0:128])
            p.copy("act", ohrep[:, tts * 128:(tts + 1) * 128], PSG[0:32, 0:128])
            p.tt("dve", zm[:, tts * 128:(tts + 1) * 128], zb[:, tts * 128:(tts + 1) * 128],
                 ohrep[:, tts * 128:(tts + 1) * 128], ALU.mult)

    ln_phase(xt, attn_sp, h1f_sp, h1b, "lo", after_tile=ffn_head)

    def lora_extra(oc, it):
        return (bstT[0:32, oc * 128:(oc + 1) * 128], zm[0:32, it * TT:(it + 1) * TT],
                None, None)

    big_mm_phase(wup_t, h1b, ffn_sp, "fs", extra_ops=lora_extra, ev_slots=sm_ev)

    ln_phase(h1f_sp, ffn_sp, None, out_fm, "o")

    p.build(final_deps=[p.chan_point(f"o{i}") for i in range(4)])
    return nc, p


def tile_w(wT, ntile):
    wT = np.asarray(wT, np.float32)
    out_dim = wT.shape[1]
    pad = ntile * 128 - out_dim
    if pad:
        wT = np.concatenate([wT, np.zeros((wT.shape[0], pad), np.float32)], axis=1)
    r = wT.reshape(32, 128, ntile, 128).transpose(2, 1, 0, 3)
    return np.ascontiguousarray(r.reshape(ntile * 128, 4096))


def make_consts():
    consts = np.zeros((128, 512), np.float32)
    ii, jj = np.arange(128)[:, None], np.arange(128)[None, :]
    consts[:, 0:128] = (ii <= jj)
    consts[:, 128:256] = np.eye(128)
    consts[0:64, 256:320] = np.eye(64)
    for e in range(4):
        consts[e, 320 + 8 * e:320 + 8 * e + 8] = 1.0
    consts[:, 352:480] = 1.0
    return consts


def kernel(**inp):
    a = {k: np.asarray(v, dtype=np.float32) for k, v in inp.items()}
    x = a["L0_attn_hidden_states"]
    B, L, DM = x.shape
    cw = a["L0_attn_conv_weight"]
    cb = a["L0_attn_conv_bias"]
    consts = make_consts()

    common = dict(
        win_t=tile_w(a["L0_attn_in_proj_w"].T, 81),
        wout_t=tile_w(a["L0_attn_out_proj_w"].T, 32),
        wup_t=tile_w(a["L0_ffn_W_up"].T, 32),
        cw_in=np.ascontiguousarray(cw.reshape(48, 128, 4).transpose(1, 0, 2).reshape(128, 192)),
        cb_in=np.ascontiguousarray(cb.reshape(48, 128).T),
        dtb=a["L0_attn_dt_bias"].reshape(1, 64),
        alog=a["L0_attn_A_log"].reshape(1, 64),
        dvec=a["L0_attn_D"].reshape(1, 64),
        nw_in=np.ascontiguousarray(np.broadcast_to(a["L0_attn_norm_weight"], (128, DM))),
        gwT=np.ascontiguousarray(a["L0_ffn_gate_w"].T),
        ebias=a["L0_ffn_expert_bias"].reshape(1, 4),
        loraA_T=np.ascontiguousarray(a["L0_ffn_lora_A"].reshape(32, DM).T),
        loraB_st=np.ascontiguousarray(a["L0_ffn_lora_B"].transpose(0, 2, 1).reshape(32, DM)),
        cst_in=consts,
    )
    nc, _ = build_mamba(L)
    in_maps = [dict(common, xt=np.ascontiguousarray(x[b].T)) for b in range(B)]
    res = run_bass_kernel_spmd(nc, in_maps, core_ids=list(range(B)))
    out = np.stack([np.ascontiguousarray(res.results[b]["out_fm"]).T for b in range(B)])
    return np.ascontiguousarray(out, dtype=np.float32)
